# version 17
# baseline (speedup 1.0000x reference)
import numpy as np
import ml_dtypes
from contextlib import ExitStack

import concourse.bass as bass
import concourse.mybir as mybir
from concourse.bass_utils import run_bass_kernel_spmd

F32 = mybir.dt.float32
BF16 = mybir.dt.bfloat16
AF = mybir.ActivationFunctionType
OP = mybir.AluOpType
AX = mybir.AxisListType

B, L, D = 2, 8192, 1024
NST = 16
TQ = 2048
HALO = 4
SPAN = TQ + HALO
DC = 8
DQ = 256
NT = 4
TW = 512
EM = 4
IF = 512
SCH = 2048
NCH = L // SCH


def build_nc(detect_races=True, stop=9, dbg=False, _qd_empty=False, _psumonly=False):
    global _QD_EMPTY, _DBG_PSUMONLY
    _QD_EMPTY = _qd_empty
    _DBG_PSUMONLY = _psumonly
    pass
    nc = bass.Bass("TRN2", target_bir_lowering=False,
                   detect_race_conditions=detect_races)

    d_x = nc.dram_tensor("x_fm", [D, SPAN], F32, kind="ExternalInput").ap()
    d_winT = nc.dram_tensor("winT", [D, D], F32, kind="ExternalInput").ap()
    d_woutT = nc.dram_tensor("woutT", [D, D], F32, kind="ExternalInput").ap()
    d_wg = nc.dram_tensor("wg", [EM, D, IF], F32, kind="ExternalInput").ap()
    d_wu = nc.dram_tensor("wu", [EM, D, IF], F32, kind="ExternalInput").ap()
    d_wd = nc.dram_tensor("wd", [EM, IF, D], F32, kind="ExternalInput").ap()
    d_wr = nc.dram_tensor("wr", [D, 8], F32, kind="ExternalInput").ap()
    d_rb = nc.dram_tensor("rbias", [1, 8], F32, kind="ExternalInput").ap()
    d_cwT = nc.dram_tensor("convwT", [D, 4], F32, kind="ExternalInput").ap()
    d_cb = nc.dram_tensor("convb", [D, 1], F32, kind="ExternalInput").ap()
    d_alog = nc.dram_tensor("alog_q", [DQ, NST], F32, kind="ExternalInput").ap()
    d_bssm = nc.dram_tensor("bssm_q", [DQ, NST], F32, kind="ExternalInput").ap()
    d_cssm = nc.dram_tensor("cssm_q", [DQ, NST], F32, kind="ExternalInput").ap()
    d_dtlog = nc.dram_tensor("dtlog_q", [DQ, 1], F32, kind="ExternalInput").ap()
    d_dssm = nc.dram_tensor("dssm_q", [DQ, 1], F32, kind="ExternalInput").ap()
    d_out = nc.dram_tensor("out_fm", [D, TQ], F32, kind="ExternalOutput").ap()

    if True:
        d_gdbg = nc.dram_tensor("g_dbg", [D, TQ], BF16, kind="ExternalOutput").ap() if dbg else None
        d_hidbg = nc.dram_tensor("hid_dbg", [128, 16 * 512], BF16, kind="ExternalOutput").ap() if dbg else None
        d_cwbdbg = nc.dram_tensor("cwbc_dbg", [128, 6 * 512], BF16, kind="ExternalOutput").ap() if dbg else None
        d_bkdbg = nc.dram_tensor("bank_dbg", [128, 2 * 512], F32, kind="ExternalOutput").ap() if dbg else None
        d_ycdbg = nc.dram_tensor("yc_dbg", [256, L], BF16, kind="ExternalOutput").ap() if dbg else None
        d_sdbg = nc.dram_tensor("s_dbg", [D, TQ], F32, kind="ExternalOutput").ap() if dbg else None
        d_ydbg = nc.dram_tensor("y_dbg", [D, TQ], BF16, kind="ExternalOutput").ap() if dbg else None
    t_id = nc.dram_tensor("id_dram", [128, 128], F32)
    t_uc = nc.dram_tensor("u_contrib", [D, TQ], BF16)
    t_ug = nc.dram_tensor("u_gath", [4 * D, TQ], BF16)
    t_yc = nc.dram_tensor("y_contrib", [4 * DQ, SCH], BF16)
    t_ygc = [nc.dram_tensor(f"y_gath{c}", [4 * DQ, SCH], BF16) for c in range(4)]

    root = ExitStack()
    with root:
        def mk(stack):
            def sb(name, shape, dt=F32):
                return stack.enter_context(nc.sbuf_tensor(name, shape, dt))
            return sb
        sbr = mk(root)

        xn_bf = [sbr(f"xnbf{i}", [128, SPAN], BF16) for i in range(DC)]
        ones_row = sbr("ones_row", [1, 128])
        ones_row_bf = sbr("ones_row_bf", [1, 128], BF16)
        ones_col = sbr("ones_col", [128, 1])
        id_sb = sbr("id_sb", [128, 128])
        wr_sb = [sbr(f"wr{i}", [128, 8]) for i in range(DC)]
        rb_row = sbr("rb_row", [1, 8])
        rb_bc = sbr("rb_bc", [128, 8])
        convw = [sbr(f"cvw{i}", [128, 4]) for i in range(DC)]
        convb = [sbr(f"cvb{i}", [128, 1]) for i in range(DC)]
        alog = [sbr(f"alog{i}", [128, NST]) for i in range(2)]
        bssm = [sbr(f"bssm{i}", [128, NST]) for i in range(2)]
        cssm = [sbr(f"cssm{i}", [128, NST]) for i in range(2)]
        dtlog = [sbr(f"dtlog{i}", [128, 1]) for i in range(2)]
        dssm = [sbr(f"dssm{i}", [128, 1]) for i in range(2)]
        abar = [sbr(f"abar{i}", [128, NST]) for i in range(2)]
        cbw = [sbr(f"cbw{i}", [128, NST]) for i in range(2)]
        prm_t = [sbr(f"prm{i}", [128, NST]) for i in range(2)]
        cwT_sb = sbr("cwT", [16, TQ], BF16)
        g_sb = [sbr(f"gg{i}", [128, TQ], BF16) for i in range(DC)]
        sst = [sbr(f"sst{i}", [128, NST], F32) for i in range(2)]

        pb = [root.enter_context(nc.psum_tensor(f"pb{i}", [128, TW], F32))
              for i in range(8)]

        sem = lambda name: root.enter_context(nc.semaphore(name))
        s_ldx = sem("s_ldx"); s_ldw = sem("s_ldw"); s_lfw = sem("s_lfw")
        s_lwo = sem("s_lwo"); s_id = sem("s_id"); s_idv = sem("s_idv"); s_prm = sem("s_prm")
        s_sq = sem("s_sq"); s_stA = sem("s_stA"); s_stB = sem("s_stB")
        s_sr = sem("s_sr"); s_row = sem("s_row"); s_bc = sem("s_bc")
        s_ln = sem("s_ln"); s_rt = sem("s_rt"); s_sg = sem("s_sg")
        s_cwm = sem("s_cwm"); s_cwd = sem("s_cwd"); s_cwe = sem("s_cwe")
        s_rbe = sem("s_rbe")
        s_u = sem("s_u"); s_ue = sem("s_ue"); s_us = sem("s_us")
        s_ag = sem("s_ag"); s_ur = sem("s_ur"); s_cb2 = sem("s_cb2")
        s_fb = sem("s_fb"); s_fe = sem("s_fe"); s_fg = sem("s_fg")
        s_fs = sem("s_fs"); s_fh = sem("s_fh"); s_fd = sem("s_fd")
        s_fc = sem("s_fc"); s_fz = sem("s_fz"); s_gs = sem("s_gs")
        s_pm = sem("s_pm"); s_sc = sem("s_sc"); s_scn = sem("s_scn")
        s_dbx = sem("s_dbx")
        s_yb = sem("s_yb"); s_ys = sem("s_ys"); s_yr = sem("s_yr")
        s_wo = sem("s_wo"); s_se = sem("s_se"); s_xr = sem("s_xr")
        s_fo = sem("s_fo"); s_o = sem("s_o"); s_ph = sem("s_ph")
        s_vg = sem("s_vg"); s_xc = sem("s_xc"); s_stC = sem("s_stC")
        s_vr = sem("s_vr"); s_sq2 = sem("s_sq2")
        s_pm2 = sem("s_pm2"); s_pa = sem("s_pa"); s_dc = sem("s_dc")
        s_uc = sem("s_uc")

        VGC = [0]

        def vg(v, instr):
            VGC[0] += 1
            instr.then_inc(s_vg, 1)
            v.wait_ge(s_vg, VGC[0])
            return instr

        TILES = [(HALO, TW), (HALO + TW, TW), (HALO + 2 * TW, TW),
                 (HALO + 3 * TW, TW), (0, HALO)]
        N_LDX = (8 + 24 + 1 + 10) * 16

        p1 = ExitStack()
        sb1 = mk(p1)
        x_sb = [sb1(f"x{i}", [128, SPAN], F32) for i in range(DC)]
        winT = [sb1(f"winT{i}", [128, D], BF16) for i in range(DC)]
        xn_fp = [sb1(f"xnfp{i}", [128, TW], F32) for i in range(DC)]
        zeros_id = sb1("zeros_id", [128, 128], F32)
        ones_bf_mat = sb1("ones_bf_mat", [128, 128], BF16)
        xb_bf = [sb1(f"xbbf{i}", [128, TW], BF16) for i in range(2)]
        sq_bf = [sb1(f"sqbf{i}", [128, TW], BF16) for i in range(2)]
        mu_bc = sb1("mu_bc", [128, TW], F32)
        rstd_bc = sb1("rstd_bc", [128, TW], F32)
        st0 = sb1("st0", [128, TW], F32)
        st1 = sb1("st1", [128, TW], F32)
        sc_sb = sb1("sc_sb", [128, 8], F32)
        rt0 = sb1("rt0", [128, 8], F32)
        rt1 = sb1("rt1", [128, 8], F32)
        col0 = sb1("col0", [128, 1], F32)
        col1 = sb1("col1", [128, 1], F32)
        cw_tm = [sb1(f"cw_tm{i}", [128, 16], F32) for i in range(2)]
        u_stage = [sb1(f"ust{i}", [128, TW], BF16) for i in range(2)]

        with nc.Block() as blk:
            @blk.sync
            def _(s):
                for i in range(DC):
                    s.dma_start(x_sb[i][:], d_x[128 * i:128 * (i + 1), :]).then_inc(s_ldx, 16)
                for i in range(DC):
                    s.dma_start(wr_sb[i][:], d_wr[128 * i:128 * (i + 1), :]).then_inc(s_ldx, 16)
                    s.dma_start(convw[i][:], d_cwT[128 * i:128 * (i + 1), :]).then_inc(s_ldx, 16)
                    s.dma_start(convb[i][:], d_cb[128 * i:128 * (i + 1), :]).then_inc(s_ldx, 16)
                s.dma_start(rb_row[:], d_rb[:]).then_inc(s_ldx, 16)
                for i in range(2):
                    s.dma_start(alog[i][:], d_alog[128 * i:128 * (i + 1), :]).then_inc(s_ldx, 16)
                    s.dma_start(bssm[i][:], d_bssm[128 * i:128 * (i + 1), :]).then_inc(s_ldx, 16)
                    s.dma_start(cssm[i][:], d_cssm[128 * i:128 * (i + 1), :]).then_inc(s_ldx, 16)
                    s.dma_start(dtlog[i][:], d_dtlog[128 * i:128 * (i + 1), :]).then_inc(s_ldx, 16)
                    s.dma_start(dssm[i][:], d_dssm[128 * i:128 * (i + 1), :]).then_inc(s_ldx, 16)

            @blk.gpsimd
            def _(g):
                for i in range(DC):
                    g.dma_start(winT[i][:], d_winT[128 * i:128 * (i + 1), :]).then_inc(s_ldw, 16)
                g.wait_ge(s_idv, 1)
                g.dma_start(t_id.ap(), zeros_id[:]).then_inc(s_id, 16)
                g.wait_ge(s_id, 16)
                diag = bass.AP(t_id, 0, [[129, 128]])
                g.dma_start(diag, ones_row[0:1, :],
                            allow_slow_non_contiguous=True).then_inc(s_id, 16)
                g.wait_ge(s_id, 32)
                g.dma_start(id_sb[:], t_id.ap()).then_inc(s_id, 16)

            @blk.scalar
            def _(sc):
                sc.wait_ge(s_ldx, N_LDX)
                for i in range(2):
                    sc.activation(dtlog[i][:], dtlog[i][:], AF.Exp)
                    sc.activation(cbw[i][:], alog[i][:], AF.Exp).then_inc(s_prm, 1)
                    sc.wait_ge(s_prm, 4 * i + 2)
                    sc.activation(abar[i][:], prm_t[i][:], AF.Exp).then_inc(s_prm, 1)
                if stop >= 2:
                    k = 0
                    for ti, (c0, w) in enumerate(TILES):
                        for dc in range(DC):
                            if k >= 2:
                                sc.wait_ge(s_stC, k - 1)
                            sc.activation(sq_bf[k % 2][:, 0:w], x_sb[dc][:, c0:c0 + w],
                                          AF.Square).then_inc(s_sq, 1)
                            k += 1
                        sc.wait_ge(s_vr, ti + 1)
                        sc.activation(st1[:, 0:w], st0[:, 0:w], AF.Sqrt).then_inc(s_sq2, 1)
                        if ti >= NT or stop < 4:
                            continue
                        for j in range(4):
                            sl = 4 * ti + j
                            sc.wait_ge(s_cwm, 7 * sl + 2)
                            if sl >= 1:
                                sc.wait_ge(s_cwd, sl)
                            sc.activation(sc_sb[:], rt0[:], AF.Sigmoid).then_inc(s_sg, 1)
                        if stop >= 5:
                            for oc in range(DC):
                                ku = 8 * ti + oc
                                bank = pb[0] if ku % 2 == 0 else pb[1]
                                sc.wait_ge(s_u, ku + 1)
                                if ku >= 2:
                                    sc.wait_ge(s_us, (ku - 1) * 16)
                                sc.activation(u_stage[ku % 2][:], bank[:],
                                              AF.Identity).then_inc(s_ue, 1)
                if stop >= 5:
                    sc.wait_ge(s_ue, 32)
                    sc.nop().then_inc(s_ph, 1)

            @blk.tensor
            def _(t):
                def pe_trans(t, sl):
                    t.wait_ge(s_cwd, sl + 1)
                    if sl % 4 < 3:
                        t.wait_ge(s_cwm, 7 * (sl + 1) + 2)
                    else:
                        t.wait_ge(s_cwm, 7 * sl + 2)
                    t.matmul(pb[4][0:16, 0:128], cw_tm[sl % 2][:, 0:16], id_sb[:],
                             start=True, stop=True).then_inc(s_rt, 1)

                def pe_uproj(t, k):
                    ti, oc = divmod(k, DC)
                    bank = pb[0] if k % 2 == 0 else pb[1]
                    if k == 0:
                        t.wait_ge(s_ldw, 8 * 16)
                    if k >= 2:
                        t.wait_ge(s_ue, k - 1)
                    for dc in range(DC):
                        mi = t.matmul(bank[:], winT[dc][:, 128 * oc:128 * (oc + 1)],
                                      xn_bf[dc][:, HALO + TW * ti:HALO + TW * (ti + 1)],
                                      start=(dc == 0), stop=(dc == DC - 1))
                    mi.then_inc(s_u, 1)

                t.wait_ge(s_ldx, N_LDX)
                t.wait_ge(s_idv, 1)
                t.matmul(pb[6][:, 0:8], ones_row[:], rb_row[:],
                         start=True, stop=True).then_inc(s_bc, 1)
                if stop >= 4:
                    t.wait_ge(s_id, 48)
                    t.wait_ge(s_rbe, 1)
                k = 0
                for ti, (c0, w) in (enumerate(TILES) if stop >= 2 else []):
                    if ti >= 1:
                        t.wait_ge(s_sr, 2 * ti)
                    for dc in range(DC):
                        t.wait_ge(s_xc, k + 1)
                        t.matmul(pb[2][:, 0:w], ones_bf_mat[:], xb_bf[k % 2][:, 0:w],
                                 start=(dc == 0), stop=(dc == DC - 1)).then_inc(s_stB, 1)
                        t.wait_ge(s_sq, k + 1)
                        t.matmul(pb[3][:, 0:w], ones_bf_mat[:], sq_bf[k % 2][:, 0:w],
                                 start=(dc == 0), stop=(dc == DC - 1)).then_inc(s_stC, 1)
                        k += 1
                    t.wait_ge(s_stC, k)
                    t.nop().then_inc(s_stA, 2)
                    if ti >= NT or stop < 4:
                        continue
                    for j in range(4):
                        sl = 4 * ti + j
                        t.wait_ge(s_ln, 8 * (ti + 1))
                        if sl >= 1:
                            t.wait_ge(s_cwm, 7 * (sl - 1) + 2)
                            t.wait_ge(s_cwe, sl if sl % 4 == 0 else sl - 1)
                        for dc in range(4):
                            mi = t.matmul(pb[4 + dc][:, 0:8],
                                          xn_fp[dc][:, 128 * j:128 * (j + 1)],
                                          wr_sb[dc][:], start=True, stop=True)
                        mi.then_inc(s_rt, 1)
                        t.wait_ge(s_cwm, 7 * sl + 1)
                        for dc in range(4, DC):
                            mi = t.matmul(pb[dc][:, 0:8],
                                          xn_fp[dc][:, 128 * j:128 * (j + 1)],
                                          wr_sb[dc][:], start=True, stop=True)
                        mi.then_inc(s_rt, 1)
                        if j >= 1:
                            pe_trans(t, sl - 1)
                    if stop >= 5:
                        pe_uproj(t, 8 * ti)
                        pe_trans(t, 4 * ti + 3)
                        for oc in range(1, DC):
                            pe_uproj(t, 8 * ti + oc)
                if stop >= 5:
                    t.wait_ge(s_u, 32)
                    t.nop().then_inc(s_ph, 1)

            @blk.vector
            def _(v):
                v.memset(ones_row[:], 1.0)
                v.memset(ones_row_bf[:], 1.0)
                v.memset(ones_col[:], 1.0)
                v.memset(ones_bf_mat[:], 1.0)
                v.memset(zeros_id[:], 0.0)
                v.memset(cw_tm[0][:, 8:16], 0.0)
                v.memset(cw_tm[1][:, 8:16], 0.0).then_inc(s_idv, 1)
                for i in range(2):
                    v.wait_ge(s_prm, 4 * i + 1)
                    v.tensor_scalar(prm_t[i][:], cbw[i][:], dtlog[i][:, 0:1], -1.0,
                                    OP.mult, OP.mult).then_inc(s_prm, 1)
                    v.wait_ge(s_prm, 4 * i + 3)
                    vg(v, v.tensor_scalar(prm_t[i][:], abar[i][:], -1.0, 1.0, OP.mult, OP.add))
                    vg(v, v.tensor_tensor(prm_t[i][:], prm_t[i][:], bssm[i][:], OP.mult))
                    vg(v, v.reciprocal(cbw[i][:], cbw[i][:]))
                    vg(v, v.tensor_tensor(prm_t[i][:], prm_t[i][:], cbw[i][:], OP.mult))
                    v.tensor_tensor(cbw[i][:], prm_t[i][:], cssm[i][:],
                                    OP.mult).then_inc(s_prm, 1)
                v.wait_ge(s_bc, 1)
                v.tensor_copy(rb_bc[:], pb[6][:, 0:8]).then_inc(s_rbe, 1)
                cwm = 0
                k = 0
                for ti, (c0, w) in (enumerate(TILES) if stop >= 2 else []):
                    for dc in range(DC):
                        if k >= 2:
                            v.wait_ge(s_stB, k - 1)
                        v.tensor_copy(xb_bf[k % 2][:, 0:w],
                                      x_sb[dc][:, c0:c0 + w]).then_inc(s_xc, 1)
                        k += 1
                    g_ = (lambda i_: vg(v, i_)) if w < 128 else (lambda i_: i_)
                    v.wait_ge(s_stA, 2 * ti + 2)
                    g_(v.tensor_scalar(mu_bc[:, 0:w], pb[2][:, 0:w], 1.0 / D, None, OP.mult))
                    v.tensor_scalar(st0[:, 0:w], pb[3][:, 0:w], 1.0 / D, None,
                                    OP.mult).then_inc(s_sr, 2)
                    if w < 128:
                        v.wait_ge(s_sr, 2 * (ti + 1))
                    g_(v.tensor_tensor(st1[:, 0:w], mu_bc[:, 0:w], mu_bc[:, 0:w], OP.mult))
                    g_(v.tensor_tensor(st0[:, 0:w], st0[:, 0:w], st1[:, 0:w], OP.subtract))
                    v.tensor_scalar(st0[:, 0:w], st0[:, 0:w], 1e-5, None,
                                    OP.add).then_inc(s_vr, 1)
                    v.wait_ge(s_sq2, ti + 1)
                    v.reciprocal(rstd_bc[:, 0:w], st1[:, 0:w]).then_inc(s_row, 1)
                    if stop < 3:
                        continue
                    if stop >= 4 and 1 <= ti <= NT:
                        v.wait_ge(s_rt, 12 * min(ti, NT) - 2)
                    for dc in range(DC):
                        dst = xn_fp[dc][:, 0:w]
                        g_(v.tensor_tensor(dst, x_sb[dc][:, c0:c0 + w], mu_bc[:, 0:w],
                                           OP.subtract))
                        g_(v.tensor_tensor(dst, dst, rstd_bc[:, 0:w], OP.mult))
                        v.tensor_copy(xn_bf[dc][:, c0:c0 + w], dst).then_inc(s_ln, 1)
                    if ti >= NT or stop < 4:
                        continue
                    _WA = lambda s: 12 * (s // 4) + (1, 3, 6, 9)[s % 4]
                    _WB = lambda s: 12 * (s // 4) + (2, 4, 7, 10)[s % 4]
                    _WT = lambda s: 12 * (s // 4) + (5, 8, 11, 12)[s % 4]
                    for j in range(4):
                        sl = 4 * ti + j
                        ctm = cw_tm[sl % 2]
                        v.wait_ge(s_rt, _WA(sl))
                        vg(v, v.tensor_copy(rt0[:], pb[4][:, 0:8]))
                        for dc in range(5, DC - 1):
                            vg(v, v.tensor_tensor(rt0[:], rt0[:], pb[dc][:, 0:8], OP.add))
                        v.tensor_tensor(rt0[:], rt0[:], pb[DC - 1][:, 0:8],
                                        OP.add).then_inc(s_cwm, 1); cwm += 1
                        v.wait_ge(s_rt, _WB(sl))
                        for dc in range(4, DC - 1):
                            vg(v, v.tensor_tensor(rt0[:], rt0[:], pb[dc][:, 0:8], OP.add))
                        v.tensor_tensor(rt0[:], rt0[:], pb[DC - 1][:, 0:8],
                                        OP.add).then_inc(s_cwm, 1); cwm += 1
                        if j >= 1:
                            v.wait_ge(s_rt, _WT(sl - 1))
                            v.tensor_copy(cwT_sb[0:16, 128 * (sl - 1):128 * sl],
                                          pb[4][0:16, 0:128]).then_inc(s_cwe, 1)
                        v.wait_ge(s_sg, sl + 1)
                        vg(v, v.tensor_tensor(rt0[:], sc_sb[:], rb_bc[:], OP.add))
                        v.reduce_max(col0[:], rt0[:], axis=AX.X).then_inc(s_cwm, 1); cwm += 1
                        v.wait_ge(s_cwm, cwm)
                        vg(v, v.tensor_scalar(rt1[:], rt0[:], col0[:, 0:1], None, OP.is_ge))
                        vg(v, v.scalar_tensor_tensor(rt1[:], rt1[:], -1e30, rt0[:], OP.mult, OP.add))
                        v.reduce_max(col1[:], rt1[:], axis=AX.X).then_inc(s_cwm, 1); cwm += 1
                        v.wait_ge(s_cwm, cwm)
                        vg(v, v.tensor_scalar(rt1[:], rt0[:], col1[:, 0:1], None, OP.is_ge))
                        if sl >= 2:
                            v.wait_ge(s_rt, _WT(sl - 2))
                        vg(v, v.tensor_tensor(ctm[:, 0:8], sc_sb[:], rt1[:], OP.mult))
                        v.reduce_sum(col0[:], ctm[:, 0:8], axis=AX.X).then_inc(s_cwm, 1); cwm += 1
                        v.wait_ge(s_cwm, cwm)
                        v.tensor_scalar(col0[:], col0[:], 1e-9, None, OP.add).then_inc(s_cwm, 1); cwm += 1
                        v.wait_ge(s_cwm, cwm)
                        v.reciprocal(col0[:], col0[:]).then_inc(s_cwm, 1); cwm += 1
                        v.wait_ge(s_cwm, cwm)
                        vg(v, v.tensor_scalar(ctm[:, 0:8], ctm[:, 0:8], col0[:, 0:1],
                                              None, OP.mult))
                        v.tensor_tensor(ctm[:, 8:9], ctm[:, 4:5], ctm[:, 6:7],
                                        OP.add).then_inc(s_cwd, 1)
                        if j == 3:
                            v.wait_ge(s_rt, _WT(sl))
                            v.tensor_copy(cwT_sb[0:16, 128 * sl:128 * (sl + 1)],
                                          pb[4][0:16, 0:128]).then_inc(s_cwe, 1)
                if stop >= 4:
                    v.wait_ge(s_cwe, 16)
                if stop >= 5:
                    v.nop().then_inc(s_ph, 1)

            @blk.sync
            def _(s):
                for k in (range(32) if stop >= 5 else []):
                    ti, oc = divmod(k, DC)
                    s.wait_ge(s_ue, k + 1)
                    if k >= 1:
                        s.wait_ge(s_us, 16 * k)
                    s.dma_start(t_uc.ap()[128 * oc:128 * (oc + 1), TW * ti:TW * (ti + 1)],
                                u_stage[k % 2][:]).then_inc(s_us, 16)

            @blk.gpsimd
            def _(g):
                if stop < 6:
                    return
                g.wait_ge(s_us, 32 * 16)
                g.collective_compute(
                    "AllGather", OP.bypass,
                    replica_groups=[[0, 1, 2, 3], [4, 5, 6, 7]],
                    ins=[t_uc.ap().opt()], outs=[t_ug.ap().opt()],
                ).then_inc(s_ag, 1)

        p1.close()

        if stop < 7:
            waits = {1: [(s_id, 48), (s_prm, 8)], 2: [(s_row, 5)],
                     3: [(s_ln, 40)], 4: [(s_cwe, 16)],
                     5: [(s_us, 32 * 16), (s_ph, 3)], 6: [(s_ag, 1), (s_ph, 3)],
                     8: [(s_ph, 6), (s_ag, 3)]}[stop]
            _emit_tail(nc, mk, root, d_out, waits)
            return nc

        p47 = ExitStack()
        sb7 = mk(p47)
        wg_t = [sb7(f"wgt{i}", [128, 8 * IF], BF16) for i in range(2)]
        wu_t = [sb7(f"wut{i}", [128, 8 * IF], BF16) for i in range(2)]
        wd_sb = [sb7(f"wds{e}", [128, 4 * D], BF16) for e in range(EM)]
        cw_stage = [sb7(f"cwst{i}", [1, TW], BF16) for i in range(2)]
        cwbc = [sb7(f"cwbc{i}", [128, TW], BF16) for i in range(6)]
        hid_sb = [sb7(f"hid{i}", [128, TW], BF16) for i in range(16)]
        sg_st = sb7("sg_st", [128, TW], BF16)
        uf_st = sb7("uf_st", [128, TW], BF16)
        ep0 = sb7("ep0", [128, TW], F32)
        ep1 = sb7("ep1", [128, TW], BF16)
        ep2 = sb7("ep2", [128, TW], BF16)
        u_ch = [sb7(f"uch{i}", [128, SCH], BF16) for i in range(2)]
        pbuf = [sb7(f"pbuf{i}", [128, SCH], BF16) for i in range(3)]
        sbf = sb7("sbf", [128, 2 * SCH], BF16)
        ya = [sb7(f"ya{i}", [128, SCH], BF16) for i in range(2)]

        def qidx(q):
            return q % 2, q // 2

        QDIST = [[], [], [], []] if _QD_EMPTY else [[], [0, 1], [2, 3, 4], [5, 6, 7]]

        with nc.Block() as blk:
            @blk.gpsimd
            def _(g):
                g.wait_ge(s_ph, 3)
                for w in range(2):
                    g.dma_start(wg_t[w % 2].rearrange("p (kc i) -> p kc i", kc=DC),
                                d_wg[w].rearrange("(kc p) i -> p kc i", p=128)
                                ).then_inc(s_lfw, 16)
                    g.dma_start(wu_t[w % 2].rearrange("p (kc i) -> p kc i", kc=DC),
                                d_wu[w].rearrange("(kc p) i -> p kc i", p=128)
                                ).then_inc(s_lfw, 16)
                for e in range(EM):
                    g.dma_start(wd_sb[e].rearrange("p (ic h) -> p ic h", ic=4),
                                d_wd[e].rearrange("(ic p) h -> p ic h", p=128)
                                ).then_inc(s_lfw, 16)
                g.wait_ge(s_cwe, 16)
                lanes = [0, 1, 2, 3, 8, 7]
                for ti in range(NT):
                    if ti >= 1:
                        g.wait_ge(s_fb, 6 * ti)
                    for j, ln in enumerate(lanes):
                        jj = 6 * ti + j
                        if jj >= 2:
                            g.wait_ge(s_fb, jj - 1)
                        if jj >= 1:
                            g.wait_ge(s_cb2, 16 * jj)
                        g.dma_start(cw_stage[j % 2][:],
                                    cwT_sb[ln:ln + 1, TW * ti:TW * (ti + 1)]
                                    ).then_inc(s_cb2, 16)
                    for e in range(EM):
                        w = 4 * ti + e
                        if w < 2:
                            continue
                        g.wait_ge(s_fg, 16 * (w - 1))
                        g.wait_ge(s_lfw, 128 + 32 * (w - 2))
                        g.dma_start(wg_t[w % 2].rearrange("p (kc i) -> p kc i", kc=DC),
                                    d_wg[e].rearrange("(kc p) i -> p kc i", p=128)
                                    ).then_inc(s_lfw, 16)
                        g.dma_start(wu_t[w % 2].rearrange("p (kc i) -> p kc i", kc=DC),
                                    d_wu[e].rearrange("(kc p) i -> p kc i", p=128)
                                    ).then_inc(s_lfw, 16)
                for c in range(NCH):
                    g.wait_ge(s_ys, 32 * (c + 1))
                    g.collective_compute(
                        "AllGather", OP.bypass,
                        replica_groups=[[0, 1, 2, 3], [4, 5, 6, 7]],
                        ins=[t_yc.ap()[DQ * c:DQ * (c + 1), :].opt()],
                        outs=[t_ygc[c].ap().opt()],
                    ).then_inc(s_ag, 1)
                if dbg:
                    g.dma_start(d_ycdbg, t_yc.ap()).then_inc(s_pa, 16)
                    g.wait_ge(s_pa, 16)

            if dbg:
                @blk.sync
                def _(s):
                    s.wait_ge(s_fe, 6)
                    for j in range(6):
                        s.dma_start(d_cwbdbg[:, TW * j:TW * (j + 1)],
                                    cwbc[j][:]).then_inc(s_dbx, 16)
                    s.wait_ge(s_fh, 32)
                    for i in range(16):
                        s.dma_start(d_hidbg[:, TW * i:TW * (i + 1)],
                                    hid_sb[i][:]).then_inc(s_dbx, 16)
                    s.wait_ge(s_dbx, 22 * 16)

            @blk.sync
            def _(s):
                s.wait_ge(s_ag, 1)
                s.wait_ge(s_ph, 3)
                pid = s.compute_val(s.partition_id())

                def rb(s, q, dq):
                    dcl, ch = qidx(q)
                    if q >= 1:
                        s.wait_ge(s_ur, 16 * q)
                    s.dma_start(
                        u_ch[q % 2][:],
                        t_ug.ap()[D * ch + DQ * dq + 128 * dcl:
                                  D * ch + DQ * dq + 128 * (dcl + 1), :],
                    ).then_inc(s_ur, 16)

                for i in s.Switch(pid, 8):
                    dq = i % 4
                    rb(s, 0, dq)
                    rb(s, 1, dq)
                    for q in range(8):
                        s.wait_ge(s_dc, q + 1)
                        if q >= 1:
                            s.wait_ge(s_ys, 16 * q)
                        dcl, ch = qidx(q)
                        s.dma_start(t_yc.ap()[DQ * ch + 128 * dcl:
                                              DQ * ch + 128 * (dcl + 1), :],
                                    ya[q % 2][:]).then_inc(s_ys, 16)
                        if q + 2 < 8:
                            rb(s, q + 2, dq)

            @blk.tensor
            def _(t):
                t.wait_ge(s_rbe, 1)
                for ti in range(NT):
                    for j in range(6):
                        bank = pb[6] if j % 2 == 0 else pb[7]
                        jj = 6 * ti + j
                        if jj >= 2:
                            t.wait_ge(s_fe, jj - 1)
                        t.wait_ge(s_cb2, 16 * (6 * ti + j + 1))
                        t.matmul(bank[:], ones_row_bf[:], cw_stage[j % 2][:],
                                 start=True, stop=True).then_inc(s_fb, 1)
                    for e in range(EM):
                        w = 4 * ti + e
                        t.wait_ge(s_lfw, 64 if w < 2 else 128 + 32 * (w - 1))
                        for ic in range(4):
                            k = ti * 16 + e * 4 + ic
                            bg = pb[(k % 2) * 2]
                            bu = pb[(k % 2) * 2 + 1]
                            if k >= 2:
                                t.wait_ge(s_fs, 2 * (k - 1))
                            for kc in range(DC):
                                t.matmul(bg[:], wg_t[w % 2][:, IF * kc + 128 * ic:
                                                            IF * kc + 128 * (ic + 1)],
                                         xn_bf[kc][:, HALO + TW * ti:HALO + TW * (ti + 1)],
                                         start=(kc == 0), stop=(kc == DC - 1))
                            for kc in range(DC):
                                mi = t.matmul(bu[:], wu_t[w % 2][:, IF * kc + 128 * ic:
                                                                 IF * kc + 128 * (ic + 1)],
                                              xn_bf[kc][:, HALO + TW * ti:HALO + TW * (ti + 1)],
                                              start=(kc == 0), stop=(kc == DC - 1))
                            mi.then_inc(s_fg, 4)
                    t.wait_ge(s_fh, 32 * (ti + 1))
                    if ti == 0:
                        t.wait_ge(s_lfw, 8 * 16)
                    for oc in range(DC):
                        k2 = ti * DC + oc
                        bank = pb[4 + (k2 % 2)]
                        if k2 >= 2:
                            t.wait_ge(s_fc, 3 * (k2 - 1))
                        first = True
                        for e in range(EM):
                            for ic in range(4):
                                mi = t.matmul(bank[:], wd_sb[e][:, D * ic + 128 * oc:
                                                                D * ic + 128 * (oc + 1)],
                                              hid_sb[e * 4 + ic][:],
                                              start=first, stop=(e == EM - 1 and ic == 3))
                                first = False
                        mi.then_inc(s_fd, 1)
                t.wait_ge(s_fd, 32)
                t.nop().then_inc(s_ph, 1)

            @blk.scalar
            def _(sc):
                sc.wait_ge(s_prm, 8)

                def act_p4(sc, q, n):
                    dcl, ch = qidx(q)
                    if n < NST:
                        kp = q * NST + n
                        if kp >= 3:
                            sc.wait_ge(s_sc, kp - 2)
                        sc.wait_ge(s_ur, 16 * (q + 1))
                        sc.activation(pbuf[kp % 3][:, :], u_ch[q % 2][:],
                                      AF.Identity,
                                      scale=cbw[dcl][:, n:n + 1]).then_inc(s_pm, 1)
                    else:
                        pass

                for ti in range(NT):
                    aq = [(q, n) for q in QDIST[ti] for n in range(NST + 1)]
                    ai = 0
                    for j in range(6):
                        bank = pb[6] if j % 2 == 0 else pb[7]
                        sc.wait_ge(s_fb, 6 * ti + j + 1)
                        sc.activation(cwbc[j][:], bank[:], AF.Identity).then_inc(s_fe, 1)
                    for e in range(EM):
                        for ic in range(4):
                            k = ti * 16 + e * 4 + ic
                            bg = pb[(k % 2) * 2]
                            bu = pb[(k % 2) * 2 + 1]
                            sc.wait_ge(s_fg, 4 * (k + 1))
                            if k >= 1:
                                sc.wait_ge(s_fh, 2 * k)
                            sc.activation(sg_st[:], bg[:], AF.Silu)
                            sc.activation(uf_st[:], bu[:], AF.Identity).then_inc(s_fs, 2)
                            for _ in range(2):
                                if ai < len(aq):
                                    act_p4(sc, *aq[ai]); ai += 1
                    for oc in range(DC):
                        k2 = ti * DC + oc
                        sc.wait_ge(s_fc, 3 * k2 + 1)
                        sc.activation(ep2[:], xn_bf[oc][:, HALO + TW * ti:HALO + TW * (ti + 1)],
                                      AF.Relu).then_inc(s_fz, 1)
                        for _ in range(2):
                            if ai < len(aq):
                                act_p4(sc, *aq[ai]); ai += 1
                        sc.wait_ge(s_fc, 3 * k2 + 3)
                        sc.activation(g_sb[oc][:, TW * ti:TW * (ti + 1)], ep0[:],
                                      AF.Silu).then_inc(s_gs, 1)
                    while ai < len(aq):
                        act_p4(sc, *aq[ai]); ai += 1
                sc.wait_ge(s_gs, 32)
                sc.nop().then_inc(s_ph, 1)

            @blk.vector
            def _(v):
                v.wait_ge(s_ldx, N_LDX)
                v.wait_ge(s_prm, 8)

                def dve_p4(v, q, n):
                    dcl, ch = qidx(q)
                    ysl = ya[q % 2][:]
                    if n < NST:
                        kp = q * NST + n
                        sb_sl = sbf[:, SCH * (kp % 2):SCH * (kp % 2 + 1)]
                        v.wait_ge(s_pm, kp + 1)
                        a_bc = bass.broadcast_tensor_aps(
                            abar[dcl][:, n:n + 1], pbuf[kp % 3][:, :])[0]
                        if ch == 0:
                            init = 0.0
                        else:
                            init = sst[dcl][:, n:n + 1]
                            v.wait_ge(s_scn, 8 * (q - 2) + n // 2 + 1)
                        v.tensor_tensor_scan(sb_sl, a_bc,
                                             pbuf[kp % 3][:], init,
                                             OP.mult, OP.add).then_inc(s_sc, 1)
                        if n % 2 == 1:
                            v.wait_ge(s_sc, kp + 1)
                            v.tensor_copy(
                                sst[dcl][:, n - 1:n + 1].rearrange("p (a b) -> p a b", b=1),
                                sbf.rearrange("p (h t) -> p h t", h=2)[:, :, SCH - 1:SCH]
                            ).then_inc(s_scn, 1)
                            v.wait_ge(s_scn, 8 * q + n // 2 + 1)
                        if n == 0:
                            if q >= 2:
                                v.wait_ge(s_ys, 16 * (q - 1))
                            v.tensor_copy(ysl, sb_sl)
                        else:
                            v.tensor_tensor(ysl, ysl, sb_sl, OP.add)
                    else:
                        v.scalar_tensor_tensor(ysl, u_ch[q % 2][:], dssm[dcl][:, 0:1],
                                               ysl, OP.mult, OP.add).then_inc(s_dc, 1)

                for ti in range(NT):
                    vq = [(q, n) for q in QDIST[ti] for n in range(NST + 1)]
                    vi = 0
                    for e in range(EM):
                        for ic in range(4):
                            k = ti * 16 + e * 4 + ic
                            v.wait_ge(s_fs, 2 * (k + 1))
                            v.wait_ge(s_fe, 6 * ti + e + 1)
                            h = hid_sb[e * 4 + ic]
                            v.tensor_tensor(h[:], sg_st[:], uf_st[:], OP.mult)
                            v.tensor_tensor(h[:], h[:], cwbc[e][:], OP.mult).then_inc(s_fh, 2)
                            for _ in range(2):
                                if vi < len(vq):
                                    dve_p4(v, *vq[vi]); vi += 1
                    for oc in range(DC):
                        k2 = ti * DC + oc
                        C0 = HALO + TW * ti
                        xb = xn_bf[oc]
                        v.wait_ge(s_fd, k2 + 1)
                        v.wait_ge(s_fe, 6 * (ti + 1))
                        if k2 >= 1:
                            v.wait_ge(s_gs, k2)
                        v.tensor_scalar(ep0[:], xb[:, C0 - 3:C0 - 3 + TW],
                                        convw[oc][:, 0:1], convb[oc][:, 0:1],
                                        OP.mult, OP.add)
                        for kk in range(1, 4):
                            v.scalar_tensor_tensor(ep0[:], xb[:, C0 - 3 + kk:C0 - 3 + kk + TW],
                                                   convw[oc][:, kk:kk + 1], ep0[:],
                                                   OP.mult, OP.add)
                        v.tensor_tensor(ep1[:], xb[:, C0:C0 + TW], cwbc[4][:],
                                        OP.mult).then_inc(s_fc, 1)
                        v.wait_ge(s_fz, k2 + 1)
                        v.tensor_tensor(ep2[:], ep2[:], cwbc[5][:], OP.mult)
                        v.tensor_tensor(ep1[:], ep1[:], ep2[:], OP.add).then_inc(s_fc, 1)
                        v.tensor_tensor(ep0[:], ep0[:], ep1[:], OP.add)
                        bank = pb[4 + (k2 % 2)]
                        if _DBG_PSUMONLY:
                            v.tensor_scalar(ep0[:], bank[:], 0.0, None, OP.add).then_inc(s_fc, 1)
                        else:
                            v.tensor_tensor(ep0[:], ep0[:], bank[:], OP.add).then_inc(s_fc, 1)
                        for _ in range(2):
                            if vi < len(vq):
                                dve_p4(v, *vq[vi]); vi += 1
                    while vi < len(vq):
                        dve_p4(v, *vq[vi]); vi += 1
                v.wait_ge(s_fc, 96)
                v.wait_ge(s_dc, 8)
                v.nop().then_inc(s_ph, 1)

        p47.close()

        if stop < 9:
            _emit_tail(nc, mk, root, d_out, [(s_ph, 6), (s_ag, 5)])
            return nc

        p6 = ExitStack()
        sb6 = mk(p6)
        woutT = [sb6(f"woT{i}", [128, D], BF16) for i in range(DC)]
        yg_sb = [sb6(f"yg{i}", [128, TQ], BF16) for i in range(DC)]
        s_out = [sb6(f"ss{i}", [128, TQ], F32) for i in range(DC)]
        xst = [sb6(f"xst{i}", [128, TW], F32) for i in range(2)]
        ost = [sb6(f"ost{i}", [128, TW], F32) for i in range(2)]

        with nc.Block() as blk:
            @blk.gpsimd
            def _(g):
                g.wait_ge(s_ph, 6)
                g.wait_ge(s_ys, 8 * 16)
                for i in range(DC):
                    g.dma_start(woutT[i][:], d_woutT[128 * i:128 * (i + 1), :]
                                ).then_inc(s_lwo, 16)

            @blk.sync
            def _(s):
                s.wait_ge(s_ph, 6)
                pid = s.compute_val(s.partition_id())
                for i in s.Switch(pid, 8):
                    q = i % 4
                    s.wait_ge(s_ag, 2 + q)
                    for dc in range(DC):
                        s.dma_start(yg_sb[dc][:],
                                    t_ygc[q].ap()[128 * dc:128 * (dc + 1), :]
                                    ).then_inc(s_yr, 16)
                for k in range(34):
                    if k >= 2:
                        s.wait_ge(s_fo, 2 * (k - 1))
                        kk = k - 2
                        ti2, oc2 = divmod(kk, DC)
                        if kk >= 1:
                            s.wait_ge(s_o, 16 * kk)
                        s.dma_start(d_out[128 * oc2:128 * (oc2 + 1),
                                          TW * ti2:TW * (ti2 + 1)],
                                    ost[kk % 2][:]).then_inc(s_o, 16)
                    if k < 32:
                        ti, oc = divmod(k, DC)
                        if k >= 1:
                            s.wait_ge(s_xr, 16 * k)
                        s.dma_start(xst[k % 2][:],
                                    d_x[128 * oc:128 * (oc + 1),
                                        HALO + TW * ti:HALO + TW * (ti + 1)]).then_inc(s_xr, 16)
                s.wait_ge(s_o, 32 * 16)
                if dbg:
                    for i in range(DC):
                        s.dma_start(d_gdbg[128 * i:128 * (i + 1), :], g_sb[i][:]).then_inc(s_o, 16)
                        s.dma_start(d_sdbg[128 * i:128 * (i + 1), :], s_out[i][:]).then_inc(s_o, 16)
                        s.dma_start(d_ydbg[128 * i:128 * (i + 1), :], yg_sb[i][:]).then_inc(s_o, 16)
                    s.wait_ge(s_o, 32 * 16 + 24 * 16)

            @blk.tensor
            def _(t):
                t.wait_ge(s_yr, 8 * 16)
                t.wait_ge(s_lwo, 8 * 16)
                for k in range(32):
                    ti, oc = divmod(k, DC)
                    bank = pb[k % 2]
                    if k >= 2:
                        t.wait_ge(s_se, k - 1)
                    for dc in range(DC):
                        mi = t.matmul(bank[:], woutT[dc][:, 128 * oc:128 * (oc + 1)],
                                      yg_sb[dc][:, TW * ti:TW * (ti + 1)],
                                      start=(dc == 0), stop=(dc == DC - 1))
                    mi.then_inc(s_wo, 1)

            @blk.scalar
            def _(sc):
                for k in range(32):
                    ti, oc = divmod(k, DC)
                    bank = pb[k % 2]
                    sc.wait_ge(s_wo, k + 1)
                    sc.activation(s_out[oc][:, TW * ti:TW * (ti + 1)], bank[:],
                                  AF.Identity).then_inc(s_se, 1)

            @blk.vector
            def _(v):
                v.wait_ge(s_gs, 32)
                for k in range(32):
                    ti, oc = divmod(k, DC)
                    v.wait_ge(s_se, k + 1)
                    v.wait_ge(s_xr, 16 * (k + 1))
                    if k >= 2:
                        v.wait_ge(s_o, 16 * (k - 1))
                    o = ost[k % 2]
                    v.tensor_tensor(o[:], s_out[oc][:, TW * ti:TW * (ti + 1)],
                                    g_sb[oc][:, TW * ti:TW * (ti + 1)], OP.mult)
                    v.tensor_tensor(o[:], o[:], xst[k % 2][:], OP.add).then_inc(s_fo, 2)

        p6.close()

    return nc


def _emit_tail(nc, mk, root, d_out, waits):
    st = ExitStack()
    sb = mk(st)
    junk = sb("junk_tail", [128, TQ], F32)
    s_fin = root.enter_context(nc.semaphore("s_fin"))
    with nc.Block() as blk:
        @blk.sync
        def _(s):
            for sm, val in waits:
                s.wait_ge(sm, val)
            for oc in range(DC):
                s.dma_start(d_out[128 * oc:128 * (oc + 1), :], junk[:]).then_inc(s_fin, 16)
            s.wait_ge(s_fin, DC * 16)
    st.close()


_NC_CACHE = None


def _get_nc():
    global _NC_CACHE
    if _NC_CACHE is None:
        _NC_CACHE = build_nc()
    return _NC_CACHE


def kernel(L0_attn_x, L0_attn_W_in, L0_attn_W_out, L0_attn_A_log, L0_attn_B_ssm,
           L0_attn_C_ssm, L0_attn_dt_log, L0_attn_D_ssm, L0_ffn_Wr,
           L0_ffn_router_bias, L0_ffn_Wg, L0_ffn_Wu, L0_ffn_Wd, L0_ffn_conv_w,
           L0_ffn_conv_b):
    x = np.asarray(L0_attn_x, dtype=np.float32)
    winT = np.ascontiguousarray(np.asarray(L0_attn_W_in, dtype=np.float32).T)
    woutT = np.ascontiguousarray(np.asarray(L0_attn_W_out, dtype=np.float32).T)
    wg = np.ascontiguousarray(np.asarray(L0_ffn_Wg, dtype=np.float32))
    wu = np.ascontiguousarray(np.asarray(L0_ffn_Wu, dtype=np.float32))
    wd = np.ascontiguousarray(np.asarray(L0_ffn_Wd, dtype=np.float32))
    wr = np.ascontiguousarray(np.asarray(L0_ffn_Wr, dtype=np.float32))
    rb = np.asarray(L0_ffn_router_bias, dtype=np.float32).reshape(1, 8)
    cwT = np.ascontiguousarray(
        np.asarray(L0_ffn_conv_w, dtype=np.float32).reshape(4, D).T)
    cb = np.asarray(L0_ffn_conv_b, dtype=np.float32).reshape(D, 1)
    alog = np.asarray(L0_attn_A_log, dtype=np.float32)
    bssm = np.asarray(L0_attn_B_ssm, dtype=np.float32)
    cssm = np.asarray(L0_attn_C_ssm, dtype=np.float32)
    dtlog = np.asarray(L0_attn_dt_log, dtype=np.float32).reshape(D, 1)
    dssm = np.asarray(L0_attn_D_ssm, dtype=np.float32).reshape(D, 1)

    in_maps = []
    for k in range(8):
        b, q = divmod(k, 4)
        t0 = TQ * q
        xs = np.zeros((SPAN, D), dtype=np.float32)
        h0 = max(0, t0 - HALO)
        xs[HALO - (t0 - h0):] = x[b, h0:t0 + TQ]
        x_fm = np.ascontiguousarray(xs.T)
        sl = slice(DQ * q, DQ * (q + 1))
        in_maps.append({
            "x_fm": x_fm,
            "winT": winT, "woutT": woutT,
            "wg": wg, "wu": wu, "wd": wd,
            "wr": wr, "rbias": rb, "convwT": cwT, "convb": cb,
            "alog_q": np.ascontiguousarray(alog[sl]),
            "bssm_q": np.ascontiguousarray(bssm[sl]),
            "cssm_q": np.ascontiguousarray(cssm[sl]),
            "dtlog_q": np.ascontiguousarray(dtlog[sl]),
            "dssm_q": np.ascontiguousarray(dssm[sl]),
        })

    nc = _get_nc()
    res = run_bass_kernel_spmd(nc, in_maps, core_ids=list(range(8)))

    out = np.empty((B, L, D), dtype=np.float32)
    for k in range(8):
        b, q = divmod(k, 4)
        o = np.asarray(res.results[k]["out_fm"])
        out[b, TQ * q:TQ * (q + 1), :] = o.T
    return out


# version 19 (final)
# speedup vs baseline: 1.1535x; 1.1535x over previous
import numpy as np
import ml_dtypes
from contextlib import ExitStack

import concourse.bass as bass
import concourse.mybir as mybir
from concourse.bass_utils import run_bass_kernel_spmd

F32 = mybir.dt.float32
BF16 = mybir.dt.bfloat16
AF = mybir.ActivationFunctionType
OP = mybir.AluOpType
AX = mybir.AxisListType

B, L, D = 2, 8192, 1024
NST = 16
TQ = 2048
HALO = 4
SPAN = TQ + HALO
DC = 8
DQ = 256
NT = 4
TW = 512
EM = 4
IF = 512
SCH = 2048
NCH = L // SCH


def build_nc(detect_races=True, stop=9, dbg=False, _qd_empty=False, _psumonly=False):
    global _QD_EMPTY, _DBG_PSUMONLY
    _QD_EMPTY = _qd_empty
    _DBG_PSUMONLY = _psumonly
    pass
    nc = bass.Bass("TRN2", target_bir_lowering=False,
                   detect_race_conditions=detect_races)

    d_x = nc.dram_tensor("x_fm", [D, SPAN], F32, kind="ExternalInput").ap()
    d_winT = nc.dram_tensor("winT", [D, D], F32, kind="ExternalInput").ap()
    d_woutT = nc.dram_tensor("woutT", [D, D], F32, kind="ExternalInput").ap()
    d_wg = nc.dram_tensor("wg", [EM, D, IF], F32, kind="ExternalInput").ap()
    d_wu = nc.dram_tensor("wu", [EM, D, IF], F32, kind="ExternalInput").ap()
    d_wd = nc.dram_tensor("wd", [EM, IF, D], F32, kind="ExternalInput").ap()
    d_wr = nc.dram_tensor("wr", [D, 8], F32, kind="ExternalInput").ap()
    d_rb = nc.dram_tensor("rbias", [1, 8], F32, kind="ExternalInput").ap()
    d_cwT = nc.dram_tensor("convwT", [D, 4], F32, kind="ExternalInput").ap()
    d_cb = nc.dram_tensor("convb", [D, 1], F32, kind="ExternalInput").ap()
    d_alog = nc.dram_tensor("alog_q", [DQ, NST], F32, kind="ExternalInput").ap()
    d_bssm = nc.dram_tensor("bssm_q", [DQ, NST], F32, kind="ExternalInput").ap()
    d_cssm = nc.dram_tensor("cssm_q", [DQ, NST], F32, kind="ExternalInput").ap()
    d_dtlog = nc.dram_tensor("dtlog_q", [DQ, 1], F32, kind="ExternalInput").ap()
    d_dssm = nc.dram_tensor("dssm_q", [DQ, 1], F32, kind="ExternalInput").ap()
    d_out = nc.dram_tensor("out_fm", [D, TQ], F32, kind="ExternalOutput").ap()

    if True:
        d_gdbg = nc.dram_tensor("g_dbg", [D, TQ], BF16, kind="ExternalOutput").ap() if dbg else None
        d_hidbg = nc.dram_tensor("hid_dbg", [128, 16 * 512], BF16, kind="ExternalOutput").ap() if dbg else None
        d_cwbdbg = nc.dram_tensor("cwbc_dbg", [128, 6 * 512], BF16, kind="ExternalOutput").ap() if dbg else None
        d_bkdbg = nc.dram_tensor("bank_dbg", [128, 2 * 512], F32, kind="ExternalOutput").ap() if dbg else None
        d_ycdbg = nc.dram_tensor("yc_dbg", [256, L], BF16, kind="ExternalOutput").ap() if dbg else None
        d_sdbg = nc.dram_tensor("s_dbg", [D, TQ], F32, kind="ExternalOutput").ap() if dbg else None
        d_ydbg = nc.dram_tensor("y_dbg", [D, TQ], BF16, kind="ExternalOutput").ap() if dbg else None
    t_id = nc.dram_tensor("id_dram", [128, 128], F32)
    t_uc = nc.dram_tensor("u_contrib", [D, TQ], BF16)
    t_ug = nc.dram_tensor("u_gath", [4 * D, TQ], BF16)
    t_yc = nc.dram_tensor("y_contrib", [4 * DQ, SCH], BF16)
    t_ygc = [nc.dram_tensor(f"y_gath{c}", [4 * DQ, SCH], BF16) for c in range(4)]

    root = ExitStack()
    with root:
        def mk(stack):
            def sb(name, shape, dt=F32):
                return stack.enter_context(nc.sbuf_tensor(name, shape, dt))
            return sb
        sbr = mk(root)

        xn_bf = [sbr(f"xnbf{i}", [128, SPAN], BF16) for i in range(DC)]
        ones_row = sbr("ones_row", [1, 128])
        ones_row_bf = sbr("ones_row_bf", [1, 128], BF16)
        ones_col = sbr("ones_col", [128, 1])
        id_sb = sbr("id_sb", [128, 128])
        wr_sb = [sbr(f"wr{i}", [128, 8]) for i in range(DC)]
        rb_row = sbr("rb_row", [1, 8])
        rb_bc = sbr("rb_bc", [128, 8])
        convw = [sbr(f"cvw{i}", [128, 4]) for i in range(DC)]
        convb = [sbr(f"cvb{i}", [128, 1]) for i in range(DC)]
        alog = [sbr(f"alog{i}", [128, NST]) for i in range(2)]
        bssm = [sbr(f"bssm{i}", [128, NST]) for i in range(2)]
        cssm = [sbr(f"cssm{i}", [128, NST]) for i in range(2)]
        dtlog = [sbr(f"dtlog{i}", [128, 1]) for i in range(2)]
        dssm = [sbr(f"dssm{i}", [128, 1]) for i in range(2)]
        abar = [sbr(f"abar{i}", [128, NST]) for i in range(2)]
        cbw = [sbr(f"cbw{i}", [128, NST]) for i in range(2)]
        prm_t = [sbr(f"prm{i}", [128, NST]) for i in range(2)]
        cwT_sb = sbr("cwT", [16, TQ], BF16)
        g_sb = [sbr(f"gg{i}", [128, TQ], BF16) for i in range(DC)]
        sst = [sbr(f"sst{i}", [128, NST], F32) for i in range(2)]

        pb = [root.enter_context(nc.psum_tensor(f"pb{i}", [128, TW], F32))
              for i in range(8)]

        sem = lambda name: root.enter_context(nc.semaphore(name))
        s_ldx = sem("s_ldx"); s_ldw = sem("s_ldw"); s_lfw = sem("s_lfw")
        s_lwo = sem("s_lwo"); s_id = sem("s_id"); s_idv = sem("s_idv"); s_prm = sem("s_prm")
        s_sq = sem("s_sq"); s_stA = sem("s_stA"); s_stB = sem("s_stB")
        s_sr = sem("s_sr"); s_row = sem("s_row"); s_bc = sem("s_bc")
        s_ln = sem("s_ln"); s_rt = sem("s_rt"); s_sg = sem("s_sg")
        s_cwm = sem("s_cwm"); s_cwd = sem("s_cwd"); s_cwe = sem("s_cwe")
        s_rbe = sem("s_rbe")
        s_u = sem("s_u"); s_ue = sem("s_ue"); s_us = sem("s_us")
        s_ag = sem("s_ag"); s_ur = sem("s_ur"); s_cb2 = sem("s_cb2")
        s_fb = sem("s_fb"); s_fe = sem("s_fe"); s_fg = sem("s_fg")
        s_fs = sem("s_fs"); s_fh = sem("s_fh"); s_fd = sem("s_fd")
        s_fc = sem("s_fc"); s_fz = sem("s_fz"); s_gs = sem("s_gs")
        s_pm = sem("s_pm"); s_sc = sem("s_sc"); s_scn = sem("s_scn")
        s_dbx = sem("s_dbx")
        s_yb = sem("s_yb"); s_ys = sem("s_ys"); s_yr = sem("s_yr")
        s_wo = sem("s_wo"); s_se = sem("s_se"); s_xr = sem("s_xr")
        s_fo = sem("s_fo"); s_o = sem("s_o"); s_ph = sem("s_ph")
        s_vg = sem("s_vg"); s_xc = sem("s_xc"); s_stC = sem("s_stC")
        s_vr = sem("s_vr"); s_sq2 = sem("s_sq2")
        s_o2 = sem("s_o2"); s_xr2 = sem("s_xr2")
        s_pm2 = sem("s_pm2"); s_pa = sem("s_pa"); s_dc = sem("s_dc")
        s_uc = sem("s_uc")

        VGC = [0]

        def vg(v, instr):
            VGC[0] += 1
            instr.then_inc(s_vg, 1)
            v.wait_ge(s_vg, VGC[0])
            return instr

        TILES = [(HALO, TW), (HALO + TW, TW), (HALO + 2 * TW, TW),
                 (HALO + 3 * TW, TW), (0, HALO)]
        N_LDX = (8 + 24 + 1 + 10) * 16

        p1 = ExitStack()
        sb1 = mk(p1)
        x_sb = [sb1(f"x{i}", [128, SPAN], F32) for i in range(DC)]
        winT = [sb1(f"winT{i}", [128, D], BF16) for i in range(DC)]
        xn_fp = [sb1(f"xnfp{i}", [128, TW], F32) for i in range(DC)]
        zeros_id = sb1("zeros_id", [128, 128], F32)
        ones_bf_mat = sb1("ones_bf_mat", [128, 128], BF16)
        xb_bf = [sb1(f"xbbf{i}", [128, TW], BF16) for i in range(2)]
        sq_bf = [sb1(f"sqbf{i}", [128, TW], BF16) for i in range(2)]
        mu_bc = sb1("mu_bc", [128, TW], F32)
        rstd_bc = sb1("rstd_bc", [128, TW], F32)
        st0 = sb1("st0", [128, TW], F32)
        st1 = sb1("st1", [128, TW], F32)
        sc_sb = sb1("sc_sb", [128, 8], F32)
        rt0 = sb1("rt0", [128, 8], F32)
        rt1 = sb1("rt1", [128, 8], F32)
        col0 = sb1("col0", [128, 1], F32)
        col1 = sb1("col1", [128, 1], F32)
        cw_tm = [sb1(f"cw_tm{i}", [128, 16], F32) for i in range(2)]
        u_stage = [sb1(f"ust{i}", [128, TW], BF16) for i in range(2)]

        with nc.Block() as blk:
            @blk.sync
            def _(s):
                for i in range(DC):
                    s.dma_start(x_sb[i][:], d_x[128 * i:128 * (i + 1), :]).then_inc(s_ldx, 16)
                for i in range(DC):
                    s.dma_start(wr_sb[i][:], d_wr[128 * i:128 * (i + 1), :]).then_inc(s_ldx, 16)
                    s.dma_start(convw[i][:], d_cwT[128 * i:128 * (i + 1), :]).then_inc(s_ldx, 16)
                    s.dma_start(convb[i][:], d_cb[128 * i:128 * (i + 1), :]).then_inc(s_ldx, 16)
                s.dma_start(rb_row[:], d_rb[:]).then_inc(s_ldx, 16)
                for i in range(2):
                    s.dma_start(alog[i][:], d_alog[128 * i:128 * (i + 1), :]).then_inc(s_ldx, 16)
                    s.dma_start(bssm[i][:], d_bssm[128 * i:128 * (i + 1), :]).then_inc(s_ldx, 16)
                    s.dma_start(cssm[i][:], d_cssm[128 * i:128 * (i + 1), :]).then_inc(s_ldx, 16)
                    s.dma_start(dtlog[i][:], d_dtlog[128 * i:128 * (i + 1), :]).then_inc(s_ldx, 16)
                    s.dma_start(dssm[i][:], d_dssm[128 * i:128 * (i + 1), :]).then_inc(s_ldx, 16)

            @blk.gpsimd
            def _(g):
                for i in range(DC):
                    g.dma_start(winT[i][:], d_winT[128 * i:128 * (i + 1), :]).then_inc(s_ldw, 16)
                g.wait_ge(s_idv, 1)
                g.dma_start(t_id.ap(), zeros_id[:]).then_inc(s_id, 16)
                g.wait_ge(s_id, 16)
                diag = bass.AP(t_id, 0, [[129, 128]])
                g.dma_start(diag, ones_row[0:1, :],
                            allow_slow_non_contiguous=True).then_inc(s_id, 16)
                g.wait_ge(s_id, 32)
                g.dma_start(id_sb[:], t_id.ap()).then_inc(s_id, 16)

            @blk.scalar
            def _(sc):
                sc.wait_ge(s_ldx, N_LDX)
                for i in range(2):
                    sc.activation(dtlog[i][:], dtlog[i][:], AF.Exp)
                    sc.activation(cbw[i][:], alog[i][:], AF.Exp).then_inc(s_prm, 1)
                    sc.wait_ge(s_prm, 4 * i + 2)
                    sc.activation(abar[i][:], prm_t[i][:], AF.Exp).then_inc(s_prm, 1)
                if stop >= 2:
                    k = 0
                    for ti, (c0, w) in enumerate(TILES):
                        for dc in range(DC):
                            if k >= 2:
                                sc.wait_ge(s_stC, k - 1)
                            sc.activation(sq_bf[k % 2][:, 0:w], x_sb[dc][:, c0:c0 + w],
                                          AF.Square).then_inc(s_sq, 1)
                            k += 1
                        sc.wait_ge(s_vr, ti + 1)
                        sc.activation(st1[:, 0:w], st0[:, 0:w], AF.Sqrt).then_inc(s_sq2, 1)
                        if ti >= NT or stop < 4:
                            continue
                        for j in range(4):
                            sl = 4 * ti + j
                            sc.wait_ge(s_cwm, 7 * sl + 2)
                            if sl >= 1:
                                sc.wait_ge(s_cwd, sl)
                            sc.activation(sc_sb[:], rt0[:], AF.Sigmoid).then_inc(s_sg, 1)
                        if stop >= 5:
                            for oc in range(DC):
                                ku = 8 * ti + oc
                                bank = pb[0] if ku % 2 == 0 else pb[1]
                                sc.wait_ge(s_u, ku + 1)
                                if ku >= 2:
                                    sc.wait_ge(s_us, (ku - 1) * 16)
                                sc.activation(u_stage[ku % 2][:], bank[:],
                                              AF.Identity).then_inc(s_ue, 1)
                if stop >= 5:
                    sc.wait_ge(s_ue, 32)
                    sc.nop().then_inc(s_ph, 1)

            @blk.tensor
            def _(t):
                def pe_trans(t, sl):
                    t.wait_ge(s_cwd, sl + 1)
                    if sl % 4 < 3:
                        t.wait_ge(s_cwm, 7 * (sl + 1) + 2)
                    else:
                        t.wait_ge(s_cwm, 7 * sl + 2)
                    t.matmul(pb[4][0:16, 0:128], cw_tm[sl % 2][:, 0:16], id_sb[:],
                             start=True, stop=True).then_inc(s_rt, 1)

                def pe_uproj(t, k):
                    ti, oc = divmod(k, DC)
                    bank = pb[0] if k % 2 == 0 else pb[1]
                    if k == 0:
                        t.wait_ge(s_ldw, 8 * 16)
                    if k >= 2:
                        t.wait_ge(s_ue, k - 1)
                    for dc in range(DC):
                        mi = t.matmul(bank[:], winT[dc][:, 128 * oc:128 * (oc + 1)],
                                      xn_bf[dc][:, HALO + TW * ti:HALO + TW * (ti + 1)],
                                      start=(dc == 0), stop=(dc == DC - 1))
                    mi.then_inc(s_u, 1)

                t.wait_ge(s_ldx, N_LDX)
                t.wait_ge(s_idv, 1)
                t.matmul(pb[6][:, 0:8], ones_row[:], rb_row[:],
                         start=True, stop=True).then_inc(s_bc, 1)
                if stop >= 4:
                    t.wait_ge(s_id, 48)
                    t.wait_ge(s_rbe, 1)
                k = 0
                for ti, (c0, w) in (enumerate(TILES) if stop >= 2 else []):
                    if ti >= 1:
                        t.wait_ge(s_sr, 2 * ti)
                    for dc in range(DC):
                        t.wait_ge(s_xc, k + 1)
                        t.matmul(pb[2][:, 0:w], ones_bf_mat[:], xb_bf[k % 2][:, 0:w],
                                 start=(dc == 0), stop=(dc == DC - 1)).then_inc(s_stB, 1)
                        t.wait_ge(s_sq, k + 1)
                        t.matmul(pb[3][:, 0:w], ones_bf_mat[:], sq_bf[k % 2][:, 0:w],
                                 start=(dc == 0), stop=(dc == DC - 1)).then_inc(s_stC, 1)
                        k += 1
                    t.wait_ge(s_stC, k)
                    t.nop().then_inc(s_stA, 2)
                    if ti >= NT or stop < 4:
                        continue
                    for j in range(4):
                        sl = 4 * ti + j
                        t.wait_ge(s_ln, 8 * (ti + 1))
                        if sl >= 1:
                            t.wait_ge(s_cwm, 7 * (sl - 1) + 2)
                            t.wait_ge(s_cwe, sl if sl % 4 == 0 else sl - 1)
                        for dc in range(4):
                            mi = t.matmul(pb[4 + dc][:, 0:8],
                                          xn_fp[dc][:, 128 * j:128 * (j + 1)],
                                          wr_sb[dc][:], start=True, stop=True)
                        mi.then_inc(s_rt, 1)
                        t.wait_ge(s_cwm, 7 * sl + 1)
                        for dc in range(4, DC):
                            mi = t.matmul(pb[dc][:, 0:8],
                                          xn_fp[dc][:, 128 * j:128 * (j + 1)],
                                          wr_sb[dc][:], start=True, stop=True)
                        mi.then_inc(s_rt, 1)
                        if j >= 1:
                            pe_trans(t, sl - 1)
                    if stop >= 5:
                        pe_uproj(t, 8 * ti)
                        pe_trans(t, 4 * ti + 3)
                        for oc in range(1, DC):
                            pe_uproj(t, 8 * ti + oc)
                if stop >= 5:
                    t.wait_ge(s_u, 32)
                    t.nop().then_inc(s_ph, 1)

            @blk.vector
            def _(v):
                v.memset(ones_row[:], 1.0)
                v.memset(ones_row_bf[:], 1.0)
                v.memset(ones_col[:], 1.0)
                v.memset(ones_bf_mat[:], 1.0)
                v.memset(zeros_id[:], 0.0)
                v.memset(cw_tm[0][:, 8:16], 0.0)
                v.memset(cw_tm[1][:, 8:16], 0.0).then_inc(s_idv, 1)
                for i in range(2):
                    v.wait_ge(s_prm, 4 * i + 1)
                    v.tensor_scalar(prm_t[i][:], cbw[i][:], dtlog[i][:, 0:1], -1.0,
                                    OP.mult, OP.mult).then_inc(s_prm, 1)
                    v.wait_ge(s_prm, 4 * i + 3)
                    vg(v, v.tensor_scalar(prm_t[i][:], abar[i][:], -1.0, 1.0, OP.mult, OP.add))
                    vg(v, v.tensor_tensor(prm_t[i][:], prm_t[i][:], bssm[i][:], OP.mult))
                    vg(v, v.reciprocal(cbw[i][:], cbw[i][:]))
                    vg(v, v.tensor_tensor(prm_t[i][:], prm_t[i][:], cbw[i][:], OP.mult))
                    v.tensor_tensor(cbw[i][:], prm_t[i][:], cssm[i][:],
                                    OP.mult).then_inc(s_prm, 1)
                v.wait_ge(s_bc, 1)
                v.tensor_copy(rb_bc[:], pb[6][:, 0:8]).then_inc(s_rbe, 1)
                cwm = 0
                k = 0
                for ti, (c0, w) in (enumerate(TILES) if stop >= 2 else []):
                    for dc in range(DC):
                        if k >= 2:
                            v.wait_ge(s_stB, k - 1)
                        v.tensor_copy(xb_bf[k % 2][:, 0:w],
                                      x_sb[dc][:, c0:c0 + w]).then_inc(s_xc, 1)
                        k += 1
                    g_ = (lambda i_: vg(v, i_)) if w < 128 else (lambda i_: i_)
                    v.wait_ge(s_stA, 2 * ti + 2)
                    g_(v.tensor_scalar(mu_bc[:, 0:w], pb[2][:, 0:w], 1.0 / D, None, OP.mult))
                    v.tensor_scalar(st0[:, 0:w], pb[3][:, 0:w], 1.0 / D, None,
                                    OP.mult).then_inc(s_sr, 2)
                    if w < 128:
                        v.wait_ge(s_sr, 2 * (ti + 1))
                    g_(v.tensor_tensor(st1[:, 0:w], mu_bc[:, 0:w], mu_bc[:, 0:w], OP.mult))
                    g_(v.tensor_tensor(st0[:, 0:w], st0[:, 0:w], st1[:, 0:w], OP.subtract))
                    v.tensor_scalar(st0[:, 0:w], st0[:, 0:w], 1e-5, None,
                                    OP.add).then_inc(s_vr, 1)
                    v.wait_ge(s_sq2, ti + 1)
                    v.reciprocal(rstd_bc[:, 0:w], st1[:, 0:w]).then_inc(s_row, 1)
                    if stop < 3:
                        continue
                    if stop >= 4 and 1 <= ti <= NT:
                        v.wait_ge(s_rt, 12 * min(ti, NT) - 2)
                    for dc in range(DC):
                        dst = xn_fp[dc][:, 0:w]
                        g_(v.tensor_tensor(dst, x_sb[dc][:, c0:c0 + w], mu_bc[:, 0:w],
                                           OP.subtract))
                        g_(v.tensor_tensor(dst, dst, rstd_bc[:, 0:w], OP.mult))
                        v.tensor_copy(xn_bf[dc][:, c0:c0 + w], dst).then_inc(s_ln, 1)
                    if ti >= NT or stop < 4:
                        continue
                    _WA = lambda s: 12 * (s // 4) + (1, 3, 6, 9)[s % 4]
                    _WB = lambda s: 12 * (s // 4) + (2, 4, 7, 10)[s % 4]
                    _WT = lambda s: 12 * (s // 4) + (5, 8, 11, 12)[s % 4]
                    for j in range(4):
                        sl = 4 * ti + j
                        ctm = cw_tm[sl % 2]
                        v.wait_ge(s_rt, _WA(sl))
                        vg(v, v.tensor_copy(rt0[:], pb[4][:, 0:8]))
                        for dc in range(5, DC - 1):
                            vg(v, v.tensor_tensor(rt0[:], rt0[:], pb[dc][:, 0:8], OP.add))
                        v.tensor_tensor(rt0[:], rt0[:], pb[DC - 1][:, 0:8],
                                        OP.add).then_inc(s_cwm, 1); cwm += 1
                        v.wait_ge(s_rt, _WB(sl))
                        for dc in range(4, DC - 1):
                            vg(v, v.tensor_tensor(rt0[:], rt0[:], pb[dc][:, 0:8], OP.add))
                        v.tensor_tensor(rt0[:], rt0[:], pb[DC - 1][:, 0:8],
                                        OP.add).then_inc(s_cwm, 1); cwm += 1
                        if j >= 1:
                            v.wait_ge(s_rt, _WT(sl - 1))
                            v.tensor_copy(cwT_sb[0:16, 128 * (sl - 1):128 * sl],
                                          pb[4][0:16, 0:128]).then_inc(s_cwe, 1)
                        v.wait_ge(s_sg, sl + 1)
                        vg(v, v.tensor_tensor(rt0[:], sc_sb[:], rb_bc[:], OP.add))
                        v.reduce_max(col0[:], rt0[:], axis=AX.X).then_inc(s_cwm, 1); cwm += 1
                        v.wait_ge(s_cwm, cwm)
                        vg(v, v.tensor_scalar(rt1[:], rt0[:], col0[:, 0:1], None, OP.is_ge))
                        vg(v, v.scalar_tensor_tensor(rt1[:], rt1[:], -1e30, rt0[:], OP.mult, OP.add))
                        v.reduce_max(col1[:], rt1[:], axis=AX.X).then_inc(s_cwm, 1); cwm += 1
                        v.wait_ge(s_cwm, cwm)
                        vg(v, v.tensor_scalar(rt1[:], rt0[:], col1[:, 0:1], None, OP.is_ge))
                        if sl >= 2:
                            v.wait_ge(s_rt, _WT(sl - 2))
                        vg(v, v.tensor_tensor(ctm[:, 0:8], sc_sb[:], rt1[:], OP.mult))
                        v.reduce_sum(col0[:], ctm[:, 0:8], axis=AX.X).then_inc(s_cwm, 1); cwm += 1
                        v.wait_ge(s_cwm, cwm)
                        v.tensor_scalar(col0[:], col0[:], 1e-9, None, OP.add).then_inc(s_cwm, 1); cwm += 1
                        v.wait_ge(s_cwm, cwm)
                        v.reciprocal(col0[:], col0[:]).then_inc(s_cwm, 1); cwm += 1
                        v.wait_ge(s_cwm, cwm)
                        vg(v, v.tensor_scalar(ctm[:, 0:8], ctm[:, 0:8], col0[:, 0:1],
                                              None, OP.mult))
                        v.tensor_tensor(ctm[:, 8:9], ctm[:, 4:5], ctm[:, 6:7],
                                        OP.add).then_inc(s_cwd, 1)
                        if j == 3:
                            v.wait_ge(s_rt, _WT(sl))
                            v.tensor_copy(cwT_sb[0:16, 128 * sl:128 * (sl + 1)],
                                          pb[4][0:16, 0:128]).then_inc(s_cwe, 1)
                if stop >= 4:
                    v.wait_ge(s_cwe, 16)
                if stop >= 5:
                    v.nop().then_inc(s_ph, 1)

            @blk.sync
            def _(s):
                for k in (range(32) if stop >= 5 else []):
                    ti, oc = divmod(k, DC)
                    s.wait_ge(s_ue, k + 1)
                    if k >= 1:
                        s.wait_ge(s_us, 16 * k)
                    s.dma_start(t_uc.ap()[128 * oc:128 * (oc + 1), TW * ti:TW * (ti + 1)],
                                u_stage[k % 2][:]).then_inc(s_us, 16)

        p1.close()

        if stop < 7:
            waits = {1: [(s_id, 48), (s_prm, 8)], 2: [(s_row, 5)],
                     3: [(s_ln, 40)], 4: [(s_cwe, 16)],
                     5: [(s_us, 32 * 16), (s_ph, 3)], 6: [(s_us, 32 * 16), (s_ph, 3)],
                     8: [(s_ph, 6), (s_ag, 3)]}[stop]
            _emit_tail(nc, mk, root, d_out, waits)
            return nc

        p47 = ExitStack()
        sb7 = mk(p47)
        wg_t = [sb7(f"wgt{i}", [128, 8 * IF], BF16) for i in range(2)]
        wu_t = [sb7(f"wut{i}", [128, 8 * IF], BF16) for i in range(2)]
        wd_sb = [sb7(f"wds{e}", [128, 4 * D], BF16) for e in range(EM)]
        cw_stage = [sb7(f"cwst{i}", [1, TW], BF16) for i in range(2)]
        cwbc = [sb7(f"cwbc{i}", [128, TW], BF16) for i in range(6)]
        hid_sb = [sb7(f"hid{i}", [128, TW], BF16) for i in range(16)]
        sg_st = sb7("sg_st", [128, TW], BF16)
        uf_st = sb7("uf_st", [128, TW], BF16)
        ep0 = sb7("ep0", [128, TW], F32)
        ep1 = sb7("ep1", [128, TW], BF16)
        ep2 = sb7("ep2", [128, TW], BF16)
        u_ch = [sb7(f"uch{i}", [128, SCH], BF16) for i in range(2)]
        pbuf = [sb7(f"pbuf{i}", [128, SCH], BF16) for i in range(3)]
        sbf = sb7("sbf", [128, 2 * SCH], BF16)
        ya = [sb7(f"ya{i}", [128, SCH], BF16) for i in range(2)]

        def qidx(q):
            return q % 2, q // 2

        QDIST = [[], [], [], []] if _QD_EMPTY else [[], [0, 1], [2, 3, 4], [5, 6, 7]]

        with nc.Block() as blk:
            @blk.gpsimd
            def _(g):
                g.wait_ge(s_sq, 40)
                g.wait_ge(s_xc, 40)
                g.wait_ge(s_ln, 40)
                for w in range(2):
                    g.dma_start(wg_t[w % 2].rearrange("p (kc i) -> p kc i", kc=DC),
                                d_wg[w].rearrange("(kc p) i -> p kc i", p=128)
                                ).then_inc(s_lfw, 16)
                    g.dma_start(wu_t[w % 2].rearrange("p (kc i) -> p kc i", kc=DC),
                                d_wu[w].rearrange("(kc p) i -> p kc i", p=128)
                                ).then_inc(s_lfw, 16)
                for e in range(EM):
                    g.dma_start(wd_sb[e].rearrange("p (ic h) -> p ic h", ic=4),
                                d_wd[e].rearrange("(ic p) h -> p ic h", p=128)
                                ).then_inc(s_lfw, 16)
                g.wait_ge(s_us, 32 * 16)
                g.collective_compute(
                    "AllGather", OP.bypass,
                    replica_groups=[[0, 1, 2, 3], [4, 5, 6, 7]],
                    ins=[t_uc.ap().opt()], outs=[t_ug.ap().opt()],
                ).then_inc(s_ag, 1)
                g.wait_ge(s_u, 32)
                g.wait_ge(s_cwe, 16)
                lanes = [0, 1, 2, 3, 8, 7]
                for ti in range(NT):
                    if ti >= 1:
                        g.wait_ge(s_fb, 6 * ti)
                    for j, ln in enumerate(lanes):
                        jj = 6 * ti + j
                        if jj >= 2:
                            g.wait_ge(s_fb, jj - 1)
                        if jj >= 1:
                            g.wait_ge(s_cb2, 16 * jj)
                        g.dma_start(cw_stage[j % 2][:],
                                    cwT_sb[ln:ln + 1, TW * ti:TW * (ti + 1)]
                                    ).then_inc(s_cb2, 16)
                    for e in range(EM):
                        w = 4 * ti + e
                        if w < 2:
                            continue
                        g.wait_ge(s_fg, 16 * (w - 1))
                        g.wait_ge(s_lfw, 128 + 32 * (w - 2))
                        g.dma_start(wg_t[w % 2].rearrange("p (kc i) -> p kc i", kc=DC),
                                    d_wg[e].rearrange("(kc p) i -> p kc i", p=128)
                                    ).then_inc(s_lfw, 16)
                        g.dma_start(wu_t[w % 2].rearrange("p (kc i) -> p kc i", kc=DC),
                                    d_wu[e].rearrange("(kc p) i -> p kc i", p=128)
                                    ).then_inc(s_lfw, 16)
                for c in range(NCH):
                    g.wait_ge(s_ys, 32 * (c + 1))
                    g.collective_compute(
                        "AllGather", OP.bypass,
                        replica_groups=[[0, 1, 2, 3], [4, 5, 6, 7]],
                        ins=[t_yc.ap()[DQ * c:DQ * (c + 1), :].opt()],
                        outs=[t_ygc[c].ap().opt()],
                    ).then_inc(s_ag, 1)
                if dbg:
                    g.dma_start(d_ycdbg, t_yc.ap()).then_inc(s_pa, 16)
                    g.wait_ge(s_pa, 16)

            if dbg:
                @blk.sync
                def _(s):
                    s.wait_ge(s_fe, 6)
                    for j in range(6):
                        s.dma_start(d_cwbdbg[:, TW * j:TW * (j + 1)],
                                    cwbc[j][:]).then_inc(s_dbx, 16)
                    s.wait_ge(s_fh, 32)
                    for i in range(16):
                        s.dma_start(d_hidbg[:, TW * i:TW * (i + 1)],
                                    hid_sb[i][:]).then_inc(s_dbx, 16)
                    s.wait_ge(s_dbx, 22 * 16)

            @blk.sync
            def _(s):
                s.wait_ge(s_ag, 1)
                s.wait_ge(s_ph, 3)
                pid = s.compute_val(s.partition_id())

                def rb(s, q, dq):
                    dcl, ch = qidx(q)
                    if q >= 1:
                        s.wait_ge(s_ur, 16 * q)
                    s.dma_start(
                        u_ch[q % 2][:],
                        t_ug.ap()[D * ch + DQ * dq + 128 * dcl:
                                  D * ch + DQ * dq + 128 * (dcl + 1), :],
                    ).then_inc(s_ur, 16)

                for i in s.Switch(pid, 8):
                    dq = i % 4
                    rb(s, 0, dq)
                    rb(s, 1, dq)
                    for q in range(8):
                        s.wait_ge(s_dc, q + 1)
                        if q >= 1:
                            s.wait_ge(s_ys, 16 * q)
                        dcl, ch = qidx(q)
                        s.dma_start(t_yc.ap()[DQ * ch + 128 * dcl:
                                              DQ * ch + 128 * (dcl + 1), :],
                                    ya[q % 2][:]).then_inc(s_ys, 16)
                        if q + 2 < 8:
                            rb(s, q + 2, dq)

            @blk.tensor
            def _(t):
                t.wait_ge(s_rbe, 1)
                for ti in range(NT):
                    for j in range(6):
                        bank = pb[6] if j % 2 == 0 else pb[7]
                        jj = 6 * ti + j
                        if jj >= 2:
                            t.wait_ge(s_fe, jj - 1)
                        t.wait_ge(s_cb2, 16 * (6 * ti + j + 1))
                        t.matmul(bank[:], ones_row_bf[:], cw_stage[j % 2][:],
                                 start=True, stop=True).then_inc(s_fb, 1)
                    for e in range(EM):
                        w = 4 * ti + e
                        t.wait_ge(s_lfw, 64 if w < 2 else 128 + 32 * (w - 1))
                        for ic in range(4):
                            k = ti * 16 + e * 4 + ic
                            bg = pb[(k % 2) * 2]
                            bu = pb[(k % 2) * 2 + 1]
                            if k >= 2:
                                t.wait_ge(s_fs, 2 * (k - 1))
                            for kc in range(DC):
                                t.matmul(bg[:], wg_t[w % 2][:, IF * kc + 128 * ic:
                                                            IF * kc + 128 * (ic + 1)],
                                         xn_bf[kc][:, HALO + TW * ti:HALO + TW * (ti + 1)],
                                         start=(kc == 0), stop=(kc == DC - 1))
                            for kc in range(DC):
                                mi = t.matmul(bu[:], wu_t[w % 2][:, IF * kc + 128 * ic:
                                                                 IF * kc + 128 * (ic + 1)],
                                              xn_bf[kc][:, HALO + TW * ti:HALO + TW * (ti + 1)],
                                              start=(kc == 0), stop=(kc == DC - 1))
                            mi.then_inc(s_fg, 4)
                    t.wait_ge(s_fh, 32 * (ti + 1))
                    if ti == 0:
                        t.wait_ge(s_lfw, 8 * 16)
                    for oc in range(DC):
                        k2 = ti * DC + oc
                        bank = pb[4 + (k2 % 2)]
                        if k2 >= 2:
                            t.wait_ge(s_fc, 3 * (k2 - 1))
                        first = True
                        for e in range(EM):
                            for ic in range(4):
                                mi = t.matmul(bank[:], wd_sb[e][:, D * ic + 128 * oc:
                                                                D * ic + 128 * (oc + 1)],
                                              hid_sb[e * 4 + ic][:],
                                              start=first, stop=(e == EM - 1 and ic == 3))
                                first = False
                        mi.then_inc(s_fd, 1)
                t.wait_ge(s_fd, 32)
                t.nop().then_inc(s_ph, 1)

            @blk.scalar
            def _(sc):
                sc.wait_ge(s_prm, 8)

                def act_p4(sc, q, n):
                    dcl, ch = qidx(q)
                    if n < NST:
                        kp = q * NST + n
                        if kp >= 3:
                            sc.wait_ge(s_sc, kp - 2)
                        sc.wait_ge(s_ur, 16 * (q + 1))
                        sc.activation(pbuf[kp % 3][:, :], u_ch[q % 2][:],
                                      AF.Identity,
                                      scale=cbw[dcl][:, n:n + 1]).then_inc(s_pm, 1)
                    else:
                        pass

                for ti in range(NT):
                    aq = [(q, n) for q in QDIST[ti] for n in range(NST + 1)]
                    ai = 0
                    for j in range(6):
                        bank = pb[6] if j % 2 == 0 else pb[7]
                        sc.wait_ge(s_fb, 6 * ti + j + 1)
                        sc.activation(cwbc[j][:], bank[:], AF.Identity).then_inc(s_fe, 1)
                    for e in range(EM):
                        for ic in range(4):
                            k = ti * 16 + e * 4 + ic
                            bg = pb[(k % 2) * 2]
                            bu = pb[(k % 2) * 2 + 1]
                            sc.wait_ge(s_fg, 4 * (k + 1))
                            if k >= 1:
                                sc.wait_ge(s_fh, 2 * k)
                            sc.activation(sg_st[:], bg[:], AF.Silu)
                            sc.activation(uf_st[:], bu[:], AF.Identity).then_inc(s_fs, 2)
                            for _ in range(2):
                                if ai < len(aq):
                                    act_p4(sc, *aq[ai]); ai += 1
                    for oc in range(DC):
                        k2 = ti * DC + oc
                        sc.wait_ge(s_fc, 3 * k2 + 1)
                        sc.activation(ep2[:], xn_bf[oc][:, HALO + TW * ti:HALO + TW * (ti + 1)],
                                      AF.Relu).then_inc(s_fz, 1)
                        for _ in range(2):
                            if ai < len(aq):
                                act_p4(sc, *aq[ai]); ai += 1
                        sc.wait_ge(s_fc, 3 * k2 + 3)
                        sc.activation(g_sb[oc][:, TW * ti:TW * (ti + 1)], ep0[:],
                                      AF.Silu).then_inc(s_gs, 1)
                    while ai < len(aq):
                        act_p4(sc, *aq[ai]); ai += 1
                sc.wait_ge(s_gs, 32)
                sc.nop().then_inc(s_ph, 1)

            @blk.vector
            def _(v):
                v.wait_ge(s_ldx, N_LDX)
                v.wait_ge(s_prm, 8)

                def dve_p4(v, q, n):
                    dcl, ch = qidx(q)
                    ysl = ya[q % 2][:]
                    if n < NST:
                        kp = q * NST + n
                        sb_sl = sbf[:, SCH * (kp % 2):SCH * (kp % 2 + 1)]
                        v.wait_ge(s_pm, kp + 1)
                        a_bc = bass.broadcast_tensor_aps(
                            abar[dcl][:, n:n + 1], pbuf[kp % 3][:, :])[0]
                        if ch == 0:
                            init = 0.0
                        else:
                            init = sst[dcl][:, n:n + 1]
                            v.wait_ge(s_scn, 8 * (q - 2) + n // 2 + 1)
                        v.tensor_tensor_scan(sb_sl, a_bc,
                                             pbuf[kp % 3][:], init,
                                             OP.mult, OP.add).then_inc(s_sc, 1)
                        if n % 2 == 1:
                            v.wait_ge(s_sc, kp + 1)
                            v.tensor_copy(
                                sst[dcl][:, n - 1:n + 1].rearrange("p (a b) -> p a b", b=1),
                                sbf.rearrange("p (h t) -> p h t", h=2)[:, :, SCH - 1:SCH]
                            ).then_inc(s_scn, 1)
                            v.wait_ge(s_scn, 8 * q + n // 2 + 1)
                        if n == 0:
                            if q >= 2:
                                v.wait_ge(s_ys, 16 * (q - 1))
                            v.tensor_copy(ysl, sb_sl)
                        else:
                            v.tensor_tensor(ysl, ysl, sb_sl, OP.add)
                    else:
                        v.scalar_tensor_tensor(ysl, u_ch[q % 2][:], dssm[dcl][:, 0:1],
                                               ysl, OP.mult, OP.add).then_inc(s_dc, 1)

                for ti in range(NT):
                    vq = [(q, n) for q in QDIST[ti] for n in range(NST + 1)]
                    vi = 0
                    for e in range(EM):
                        for ic in range(4):
                            k = ti * 16 + e * 4 + ic
                            v.wait_ge(s_fs, 2 * (k + 1))
                            v.wait_ge(s_fe, 6 * ti + e + 1)
                            h = hid_sb[e * 4 + ic]
                            v.tensor_tensor(h[:], sg_st[:], uf_st[:], OP.mult)
                            v.tensor_tensor(h[:], h[:], cwbc[e][:], OP.mult).then_inc(s_fh, 2)
                            for _ in range(2):
                                if vi < len(vq):
                                    dve_p4(v, *vq[vi]); vi += 1
                    for oc in range(DC):
                        k2 = ti * DC + oc
                        C0 = HALO + TW * ti
                        xb = xn_bf[oc]
                        v.wait_ge(s_fd, k2 + 1)
                        v.wait_ge(s_fe, 6 * (ti + 1))
                        if k2 >= 1:
                            v.wait_ge(s_gs, k2)
                        v.tensor_scalar(ep0[:], xb[:, C0 - 3:C0 - 3 + TW],
                                        convw[oc][:, 0:1], convb[oc][:, 0:1],
                                        OP.mult, OP.add)
                        for kk in range(1, 4):
                            v.scalar_tensor_tensor(ep0[:], xb[:, C0 - 3 + kk:C0 - 3 + kk + TW],
                                                   convw[oc][:, kk:kk + 1], ep0[:],
                                                   OP.mult, OP.add)
                        v.tensor_tensor(ep1[:], xb[:, C0:C0 + TW], cwbc[4][:],
                                        OP.mult).then_inc(s_fc, 1)
                        v.wait_ge(s_fz, k2 + 1)
                        v.tensor_tensor(ep2[:], ep2[:], cwbc[5][:], OP.mult)
                        v.tensor_tensor(ep1[:], ep1[:], ep2[:], OP.add).then_inc(s_fc, 1)
                        v.tensor_tensor(ep0[:], ep0[:], ep1[:], OP.add)
                        bank = pb[4 + (k2 % 2)]
                        if _DBG_PSUMONLY:
                            v.tensor_scalar(ep0[:], bank[:], 0.0, None, OP.add).then_inc(s_fc, 1)
                        else:
                            v.tensor_tensor(ep0[:], ep0[:], bank[:], OP.add).then_inc(s_fc, 1)
                        for _ in range(2):
                            if vi < len(vq):
                                dve_p4(v, *vq[vi]); vi += 1
                    while vi < len(vq):
                        dve_p4(v, *vq[vi]); vi += 1
                v.wait_ge(s_fc, 96)
                v.wait_ge(s_dc, 8)
                v.nop().then_inc(s_ph, 1)

        p47.close()

        if stop < 9:
            _emit_tail(nc, mk, root, d_out, [(s_ph, 6), (s_ag, 5)])
            return nc

        p6 = ExitStack()
        sb6 = mk(p6)
        woutT = [sb6(f"woT{i}", [128, D], BF16) for i in range(DC)]
        yg_sb = [sb6(f"yg{i}", [128, TQ], BF16) for i in range(DC)]
        s_out = [sb6(f"ss{i}", [128, TQ], F32) for i in range(DC)]
        xst = [sb6(f"xst{i}", [128, TW], F32) for i in range(4)]
        ost = [sb6(f"ost{i}", [128, TW], F32) for i in range(4)]

        with nc.Block() as blk:
            @blk.gpsimd
            def _(g):
                g.wait_ge(s_ph, 6)
                g.wait_ge(s_ys, 8 * 16)
                for i in range(DC):
                    g.dma_start(woutT[i][:], d_woutT[128 * i:128 * (i + 1), :]
                                ).then_inc(s_lwo, 16)

            @blk.sync
            def _(s):
                s.wait_ge(s_ph, 6)
                pid = s.compute_val(s.partition_id())
                for i in s.Switch(pid, 8):
                    q = i % 4
                    s.wait_ge(s_ag, 2 + q)
                    for dc in range(DC):
                        s.dma_start(yg_sb[dc][:],
                                    t_ygc[q].ap()[128 * dc:128 * (dc + 1), :]
                                    ).then_inc(s_yr, 16)
                for k in range(34):
                    if k >= 2:
                        s.wait_ge(s_fo, 2 * (k - 1))
                        kk = k - 2
                        ti2, oc2 = divmod(kk, DC)
                        oS = (s_o, s_o2)[kk % 2]
                        if kk >= 2:
                            s.wait_ge(oS, 16 * (kk // 2))
                        s.dma_start(d_out[128 * oc2:128 * (oc2 + 1),
                                          TW * ti2:TW * (ti2 + 1)],
                                    ost[kk % 4][:]).then_inc(oS, 16)
                    if k < 32:
                        ti, oc = divmod(k, DC)
                        xS = (s_xr, s_xr2)[k % 2]
                        if k >= 2:
                            s.wait_ge(xS, 16 * (k // 2))
                        s.dma_start(xst[k % 4][:],
                                    d_x[128 * oc:128 * (oc + 1),
                                        HALO + TW * ti:HALO + TW * (ti + 1)]).then_inc(xS, 16)
                s.wait_ge(s_o, 16 * 16)
                s.wait_ge(s_o2, 16 * 16)
                if dbg:
                    for i in range(DC):
                        s.dma_start(d_gdbg[128 * i:128 * (i + 1), :], g_sb[i][:]).then_inc(s_o, 16)
                        s.dma_start(d_sdbg[128 * i:128 * (i + 1), :], s_out[i][:]).then_inc(s_o, 16)
                        s.dma_start(d_ydbg[128 * i:128 * (i + 1), :], yg_sb[i][:]).then_inc(s_o, 16)
                    s.wait_ge(s_o, 32 * 16 + 24 * 16)

            @blk.tensor
            def _(t):
                t.wait_ge(s_yr, 8 * 16)
                t.wait_ge(s_lwo, 8 * 16)
                for k in range(32):
                    ti, oc = divmod(k, DC)
                    bank = pb[k % 2]
                    if k >= 2:
                        t.wait_ge(s_se, k - 1)
                    for dc in range(DC):
                        mi = t.matmul(bank[:], woutT[dc][:, 128 * oc:128 * (oc + 1)],
                                      yg_sb[dc][:, TW * ti:TW * (ti + 1)],
                                      start=(dc == 0), stop=(dc == DC - 1))
                    mi.then_inc(s_wo, 1)

            @blk.scalar
            def _(sc):
                for k in range(32):
                    ti, oc = divmod(k, DC)
                    bank = pb[k % 2]
                    sc.wait_ge(s_wo, k + 1)
                    sc.activation(s_out[oc][:, TW * ti:TW * (ti + 1)], bank[:],
                                  AF.Identity).then_inc(s_se, 1)

            @blk.vector
            def _(v):
                v.wait_ge(s_gs, 32)
                for k in range(32):
                    ti, oc = divmod(k, DC)
                    v.wait_ge(s_se, k + 1)
                    v.wait_ge((s_xr, s_xr2)[k % 2], 16 * (k // 2 + 1))
                    if k >= 4:
                        v.wait_ge((s_o, s_o2)[k % 2], 16 * (k // 2 - 1))
                    o = ost[k % 4]
                    v.tensor_tensor(o[:], s_out[oc][:, TW * ti:TW * (ti + 1)],
                                    g_sb[oc][:, TW * ti:TW * (ti + 1)], OP.mult)
                    v.tensor_tensor(o[:], o[:], xst[k % 4][:], OP.add).then_inc(s_fo, 2)

        p6.close()

    return nc


def _emit_tail(nc, mk, root, d_out, waits):
    st = ExitStack()
    sb = mk(st)
    junk = sb("junk_tail", [128, TQ], F32)
    s_fin = root.enter_context(nc.semaphore("s_fin"))
    with nc.Block() as blk:
        @blk.sync
        def _(s):
            for sm, val in waits:
                s.wait_ge(sm, val)
            for oc in range(DC):
                s.dma_start(d_out[128 * oc:128 * (oc + 1), :], junk[:]).then_inc(s_fin, 16)
            s.wait_ge(s_fin, DC * 16)
    st.close()


_NC_CACHE = None


def _get_nc():
    global _NC_CACHE
    if _NC_CACHE is None:
        _NC_CACHE = build_nc()
    return _NC_CACHE


def kernel(L0_attn_x, L0_attn_W_in, L0_attn_W_out, L0_attn_A_log, L0_attn_B_ssm,
           L0_attn_C_ssm, L0_attn_dt_log, L0_attn_D_ssm, L0_ffn_Wr,
           L0_ffn_router_bias, L0_ffn_Wg, L0_ffn_Wu, L0_ffn_Wd, L0_ffn_conv_w,
           L0_ffn_conv_b):
    x = np.asarray(L0_attn_x, dtype=np.float32)
    winT = np.ascontiguousarray(np.asarray(L0_attn_W_in, dtype=np.float32).T)
    woutT = np.ascontiguousarray(np.asarray(L0_attn_W_out, dtype=np.float32).T)
    wg = np.ascontiguousarray(np.asarray(L0_ffn_Wg, dtype=np.float32))
    wu = np.ascontiguousarray(np.asarray(L0_ffn_Wu, dtype=np.float32))
    wd = np.ascontiguousarray(np.asarray(L0_ffn_Wd, dtype=np.float32))
    wr = np.ascontiguousarray(np.asarray(L0_ffn_Wr, dtype=np.float32))
    rb = np.asarray(L0_ffn_router_bias, dtype=np.float32).reshape(1, 8)
    cwT = np.ascontiguousarray(
        np.asarray(L0_ffn_conv_w, dtype=np.float32).reshape(4, D).T)
    cb = np.asarray(L0_ffn_conv_b, dtype=np.float32).reshape(D, 1)
    alog = np.asarray(L0_attn_A_log, dtype=np.float32)
    bssm = np.asarray(L0_attn_B_ssm, dtype=np.float32)
    cssm = np.asarray(L0_attn_C_ssm, dtype=np.float32)
    dtlog = np.asarray(L0_attn_dt_log, dtype=np.float32).reshape(D, 1)
    dssm = np.asarray(L0_attn_D_ssm, dtype=np.float32).reshape(D, 1)

    in_maps = []
    for k in range(8):
        b, q = divmod(k, 4)
        t0 = TQ * q
        xs = np.zeros((SPAN, D), dtype=np.float32)
        h0 = max(0, t0 - HALO)
        xs[HALO - (t0 - h0):] = x[b, h0:t0 + TQ]
        x_fm = np.ascontiguousarray(xs.T)
        sl = slice(DQ * q, DQ * (q + 1))
        in_maps.append({
            "x_fm": x_fm,
            "winT": winT, "woutT": woutT,
            "wg": wg, "wu": wu, "wd": wd,
            "wr": wr, "rbias": rb, "convwT": cwT, "convb": cb,
            "alog_q": np.ascontiguousarray(alog[sl]),
            "bssm_q": np.ascontiguousarray(bssm[sl]),
            "cssm_q": np.ascontiguousarray(cssm[sl]),
            "dtlog_q": np.ascontiguousarray(dtlog[sl]),
            "dssm_q": np.ascontiguousarray(dssm[sl]),
        })

    nc = _get_nc()
    res = run_bass_kernel_spmd(nc, in_maps, core_ids=list(range(8)))

    out = np.empty((B, L, D), dtype=np.float32)
    for k in range(8):
        b, q = divmod(k, 4)
        o = np.asarray(res.results[k]["out_fm"])
        out[b, TQ * q:TQ * (q + 1), :] = o.T
    return out
